# version 11 (final)
# baseline (speedup 1.0000x reference)
import contextlib
import numpy as np
import ml_dtypes
import concourse.bass as bass
import concourse.mybir as mybir
from concourse.bass_utils import run_bass_kernel_spmd

F32 = mybir.dt.float32
BF16 = mybir.dt.bfloat16
FP8 = mybir.dt.float8e4
AF = mybir.ActivationFunctionType
ALU = mybir.AluOpType
AX = mybir.AxisListType

D = 2048
NCH = 16
NE = 4
EPS = 1e-5
FP8_MAX = 448.0
SCALE_EPS = 1e-7

ALL8 = [list(range(8))]
PAIRS = [[0, 1], [2, 3], [4, 5], [6, 7]]

def _ring_names(base, n):
    return [f"{base}_{j}" for j in range(n)]


RING_SPECS = {
    "p1dX": 3,
    "f1dW": 2, "f1dX": 5, "f1dS": 2,
    "f2dW": 6, "f2dX": 5, "f2dS": 2,
    "p3dX": 2,
    "p4dW": 2, "p4dP": 2, "p4dR": 2,
    "p5dLa": 4, "p5dLb": 4, "p5dS": 2,
    "p6dW": 4, "p6dV": 4, "p6dF": 2, "p6dX": 2, "p6dS": 2,
    "w6dW": 2, "w6dS": 4, "wfdS": 2, "dOx": 2,
}

SEM_NAMES = (
    ["seqV", "seqA", "seqG", "gC", "tT", "cc", "dG", "dC", "dO"]
    + [p + s for p in ("f1", "f2") for s in ("tP", "tM")]
    + ["p3tP", "p4tM", "p6tY", "p6tM"]
    + [n for base, k in RING_SPECS.items() for n in _ring_names(base, k)]
)


class DmaRing:

    def __init__(self, c, base, nslots, inc_per=16):
        self.c = c
        self.base = base
        self.n = nslots
        self.inc_per = inc_per
        self.issued = 0

    def sem(self, i):
        return f"{self.base}_{i % self.n}"

    def issue(self, builder, out, in_, **kw):
        i = self.issued
        self.c.inc(builder.dma_start(out=out, in_=in_, **kw), self.sem(i),
                   self.inc_per)
        self.issued += 1
        return i

    def wait(self, eng, i):
        if i < 0:
            return
        eng.wait_ge(self.c.sems[self.sem(i)], self.inc_per * (i // self.n + 1))

    def wait_all(self, eng):
        for j in range(self.n):
            tot = self.inc_per * ((self.issued - 1 - j) // self.n + 1) \
                if self.issued > j else 0
            if tot > 0:
                eng.wait_ge(self.c.sems[f"{self.base}_{j}"], tot)

_CHAIN_EXCLUDE = {"wait_ge", "dma_start", "dma_start_transpose", "collective_compute",
                  "indirect_dma_start", "nop", "engine", "bass", "sem_inc", "sem_clear"}


class _Chained:
    def __init__(self, c, eng, sem):
        object.__setattr__(self, "_c", c)
        object.__setattr__(self, "_e", eng)
        object.__setattr__(self, "_s", sem)

    def __getattr__(self, name):
        f = getattr(self._e, name)
        if name in _CHAIN_EXCLUDE or not callable(f):
            return f
        c, e, s = self._c, self._e, self._s

        def g(*a, **kw):
            if c.cnt[s] > 0:
                e.wait_ge(c.sems[s], c.cnt[s])
            instr = f(*a, **kw)
            c.inc(instr, s, 1)
            return instr

        return g


class Ctx:
    def __init__(self, nc, sems, marks, resolve):
        self.nc = nc
        self.sems = sems
        self.cnt = {k: 0 for k in sems}
        self.marks = marks
        self.resolve = resolve
        self._bar = []

    def inc(self, instr, name, n=1):
        instr.then_inc(self.sems[name], n)
        self.cnt[name] += n
        return (name, self.cnt[name])

    def tok(self, name):
        return (name, self.cnt[name])

    def wait(self, eng, token):
        if token is not None and token[1] > 0:
            eng.wait_ge(self.sems[token[0]], token[1])

    def mark(self, key, sem):
        if not self.resolve:
            self.marks[key] = (sem, self.cnt[sem])

    def wmark(self, eng, key):
        if self.resolve:
            sem, cnt = self.marks[key]
            if cnt > 0:
                eng.wait_ge(self.sems[sem], cnt)

    def V(self, eng):
        return _Chained(self, eng, "seqV")

    def A(self, eng):
        return _Chained(self, eng, "seqA")

    def G(self, eng):
        return _Chained(self, eng, "seqG")

    def snap_barrier(self):
        self._bar = [(k, v) for k, v in self.cnt.items() if v > 0]

    def wait_barrier(self, eng):
        for k, v in self._bar:
            eng.wait_ge(self.sems[k], v)


PASS_CNTS = []


def build_nc(stage="full", TL=2048):
    marks = {}
    PASS_CNTS.clear()
    _build(stage, TL, marks, resolve=False)
    return _build(stage, TL, marks, resolve=True)


def _build(stage, TL, marks, resolve):
    NT = TL // 128
    TH = TL // 2
    NTH = NT // 2
    THC = TH // 512
    TC = TL // 512
    assert TH % 512 == 0

    nc = bass.Bass("TRN2", target_bir_lowering=False, debug=False)

    X = nc.dram_tensor("x", [TL, D], F32, kind="ExternalInput").ap()
    WKB = nc.dram_tensor("wkb", [NCH, 128, NCH * 128], F32, kind="ExternalInput").ap()
    WVB = nc.dram_tensor("wvb", [NCH, 128, NCH * 128], F32, kind="ExternalInput").ap()
    WRB = nc.dram_tensor("wrb", [NCH, 128, NCH * 128], F32, kind="ExternalInput").ap()
    WOB = nc.dram_tensor("wob", [128, NCH, D], F32, kind="ExternalInput").ap()
    WFB = nc.dram_tensor("wfb", [128, NCH, D], F32, kind="ExternalInput").ap()
    DEC = nc.dram_tensor("dec", [1, D], F32, kind="ExternalInput").ap()
    GNW = nc.dram_tensor("gnw", [16, 128], F32, kind="ExternalInput").ap()
    GNB = nc.dram_tensor("gnb", [16, 128], F32, kind="ExternalInput").ap()
    BIA = nc.dram_tensor("bia", [1, D], F32, kind="ExternalInput").ap()
    WS = nc.dram_tensor("ws", [1, 1], F32, kind="ExternalInput").ap()
    HBASE = nc.dram_tensor("hbase", [128, 1], F32, kind="ExternalInput").ap()
    ISODD = nc.dram_tensor("isodd", [128, 1], F32, kind="ExternalInput").ap()
    ISEVEN = nc.dram_tensor("iseven", [128, 1], F32, kind="ExternalInput").ap()
    EYE32 = nc.dram_tensor("eye32", [128, 128], F32, kind="ExternalInput").ap()
    EYE16 = nc.dram_tensor("eye16", [128, 128], BF16, kind="ExternalInput").ap()
    ONEROW = nc.dram_tensor("onerow", [1, 128], F32, kind="ExternalInput").ap()
    ONECOL = nc.dram_tensor("onecol", [128, 1], F32, kind="ExternalInput").ap()

    Y = nc.dram_tensor("y", [TL, D], F32, kind="ExternalOutput").ap()

    s_x1 = nc.dram_tensor("s_x1", [TL, D], F32).ap()
    s_x2 = nc.dram_tensor("s_x2", [TL, D], F32).ap()
    s_state = nc.dram_tensor("s_state", [D, TL], F32).ap()
    s_rsig = nc.dram_tensor("s_rsig", [D, TL], F32).ap()
    s_out = nc.dram_tensor("s_out", [NCH, 128, TL], F32).ap()
    s_ld = nc.dram_tensor("s_ld", [16, 128], F32).ap()
    s_wohi = nc.dram_tensor("s_wohi", [128, NCH * D], BF16).ap()
    s_wolo = nc.dram_tensor("s_wolo", [128, NCH * D], BF16).ap()
    s_wf16 = nc.dram_tensor("s_wf16", [128, NCH * D], BF16).ap()
    cmax_i = nc.dram_tensor("cmax_i", [1, 1], F32)
    cmax_o = nc.dram_tensor("cmax_o", [1, 1], F32)
    ccar_i = nc.dram_tensor("ccar_i", [16, 128], F32)
    ccar_o = nc.dram_tensor("ccar_o", [16, 128], F32)
    cstat_i = nc.dram_tensor("cstat_i", [1, 2], F32)
    cstat_o = nc.dram_tensor("cstat_o", [1, 2], F32)
    cmax2_i = nc.dram_tensor("cmax2_i", [1, 1], F32)
    cmax2_o = nc.dram_tensor("cmax2_o", [1, 1], F32)

    es = contextlib.ExitStack()
    with es:
        sems = {n: es.enter_context(nc.semaphore(n)) for n in SEM_NAMES}
        block = es.enter_context(nc.Block())
        c = Ctx(nc, sems, marks, resolve)
        R = {base: DmaRing(c, base, k) for base, k in RING_SPECS.items()}
        c.R = R
        dims = dict(NT=NT, TH=TH, NTH=NTH, THC=THC, TC=TC)

        pA = nc.alloc_psum_tensor("pA", [128, 512], F32)
        pB = nc.alloc_psum_tensor("pB", [128, 512], F32)
        pC = nc.alloc_psum_tensor("pC", [128, 512], F32)
        pD = nc.alloc_psum_tensor("pD", [128, 512], F32)
        pE = nc.alloc_psum_tensor("pE", [128, 512], F32)
        pF = nc.alloc_psum_tensor("pF", [128, 512], F32)
        pT8a = nc.alloc_psum_tensor("pT8a", [128, 2048], FP8)
        pT8b = nc.alloc_psum_tensor("pT8b", [128, 2048], FP8)
        PSY = (pA, pB, pC, pD)

        eye32 = nc.alloc_sbuf_tensor("sb_eye32", [128, 128], F32)
        eye16 = nc.alloc_sbuf_tensor("sb_eye16", [128, 128], BF16)
        e8f = nc.alloc_sbuf_tensor("sb_e8", [128, 128], FP8)
        onerow = nc.alloc_sbuf_tensor("sb_onerow", [1, 128], F32)
        onecol = nc.alloc_sbuf_tensor("sb_onecol", [128, 1], F32)
        hbase = nc.alloc_sbuf_tensor("sb_hbase", [128, 1], F32)
        isodd = nc.alloc_sbuf_tensor("sb_isodd", [128, 1], F32)
        iseven = nc.alloc_sbuf_tensor("sb_iseven", [128, 1], F32)
        ld_fm = nc.alloc_sbuf_tensor("ld_fm", [128, 16], F32)
        gnw_fm = nc.alloc_sbuf_tensor("gnw_fm", [128, 16], F32)
        gnb_fm = nc.alloc_sbuf_tensor("gnb_fm", [128, 16], F32)
        mv1 = nc.alloc_sbuf_tensor("mv1", [128, 2 * NT], F32)
        mv2 = nc.alloc_sbuf_tensor("mv2", [128, 2 * NT], F32)
        mv3 = nc.alloc_sbuf_tensor("mv3", [128, 2 * NT], F32)
        am1 = nc.alloc_sbuf_tensor("am1", [128, NT], F32)
        am3 = nc.alloc_sbuf_tensor("am3", [128, NT], F32)
        rs1 = nc.alloc_sbuf_tensor("rs1", [128, NT], F32)
        rs2 = nc.alloc_sbuf_tensor("rs2", [128, NT], F32)
        rs3 = nc.alloc_sbuf_tensor("rs3", [128, NT], F32)
        nm1 = nc.alloc_sbuf_tensor("nm1", [128, NT], F32)
        nm2 = nc.alloc_sbuf_tensor("nm2", [128, NT], F32)
        nm3 = nc.alloc_sbuf_tensor("nm3", [128, NT], F32)
        qa1 = nc.alloc_sbuf_tensor("qa1", [128, NT], F32)
        qb1 = nc.alloc_sbuf_tensor("qb1", [128, NT], F32)
        qa3 = nc.alloc_sbuf_tensor("qa3", [128, NT], F32)
        qb3 = nc.alloc_sbuf_tensor("qb3", [128, NT], F32)
        sc11 = nc.alloc_sbuf_tensor("sc11", [1, 2], F32)
        sc12 = nc.alloc_sbuf_tensor("sc12", [1, 2], F32)
        carry_loc = nc.alloc_sbuf_tensor("carry_loc", [128, 16], F32)
        carry_tot = nc.alloc_sbuf_tensor("carry_tot", [128, 16], F32)
        carry_eff = nc.alloc_sbuf_tensor("carry_eff", [128, 16], F32)
        gnsum = nc.alloc_sbuf_tensor("gnsum", [128, 16], F32)
        gnsq = nc.alloc_sbuf_tensor("gnsq", [128, 16], F32)
        afm = nc.alloc_sbuf_tensor("afm", [128, 16], F32)
        bfm = nc.alloc_sbuf_tensor("bfm", [128, 16], F32)
        bfmh = nc.alloc_sbuf_tensor("bfmh", [128, 16], BF16)
        bfml = nc.alloc_sbuf_tensor("bfml", [128, 16], BF16)
        col1 = nc.alloc_sbuf_tensor("col1", [128, 1], F32)
        col2 = nc.alloc_sbuf_tensor("col2", [128, 1], F32)
        accA = nc.alloc_sbuf_tensor("accA", [128, 1], F32)
        accB = nc.alloc_sbuf_tensor("accB", [128, 1], F32)
        bnb = nc.alloc_sbuf_tensor("bnb", [128, 24], F32)

        es0 = contextlib.ExitStack()
        vrow = es0.enter_context(nc.sbuf_tensor("vrow", [1, D], F32))
        vrow2 = es0.enter_context(nc.sbuf_tensor("vrow2", [1, D], F32))

        @block.sync
        def _(sy):
            for dst, src in [(eye32[:, :], EYE32), (eye16[:, :], EYE16),
                             (onerow[:, :], ONEROW), (onecol[:, :], ONECOL),
                             (hbase[:, :], HBASE), (isodd[:, :], ISODD),
                             (iseven[:, :], ISEVEN), (vrow[:, :], DEC)]:
                c.inc(sy.dma_start(out=dst, in_=src), "dC", 16)
        t_consts = ("dC", 8 * 16)

        @block.gpsimd
        def _(g):
            g = c.G(g)
            g.memset(gnsum[:, :], 0.0)
            g.memset(gnsq[:, :], 0.0)
            c.inc(g.dma_start(out=gnw_fm[:, :], in_=GNW.rearrange("c p -> p c"),
                              allow_slow_non_contiguous=True), "dG", 16)
            c.inc(g.dma_start(out=gnb_fm[:, :], in_=GNB.rearrange("c p -> p c"),
                              allow_slow_non_contiguous=True), "dG", 16)
            c.wmark(g, "ld0")
            c.inc(g.dma_start(out=s_ld.rearrange("c p -> (c p)")[None, :],
                              in_=vrow2[:, :]), "dG", 16)
            g.wait_ge(sems["dG"], c.cnt["dG"])
            c.inc(g.dma_start(out=ld_fm[:, :], in_=s_ld.rearrange("c p -> p c"),
                              allow_slow_non_contiguous=True), "dG", 16)

        @block.scalar
        def _(s):
            s = c.A(s)
            c.wait(s, t_consts)
            s.activation(e8f[:, :], eye32[:, :], AF.Identity)
            s.activation(vrow2[:, :], vrow[:, :], AF.Tanh, scale=0.5)
            c.mark("tanh0", "seqA")
            c.wmark(s, "sig0")
            s.activation(vrow2[:, :], vrow2[:, :], AF.Ln)
            c.mark("ld0", "seqA")

        @block.vector
        def _(v):
            v = c.V(v)
            c.wait(v, t_consts)
            c.wmark(v, "tanh0")
            v.tensor_scalar(vrow2[:, :], vrow2[:, :], 0.5, 0.5, ALU.mult, ALU.add)
            v.tensor_scalar(vrow2[:, :], vrow2[:, :], SCALE_EPS, None, ALU.max)
            c.mark("sig0", "seqV")

        c.snap_barrier()
        es0.close()

        do_p6 = stage in ("full", "wo")
        f1w_es = contextlib.ExitStack()
        f1wfb = f1w_es.enter_context(nc.sbuf_tensor("f1wf", [128, NCH * D], BF16))
        f1wfss = [f1w_es.enter_context(nc.sbuf_tensor(f"f1wfs{j}", [128, D], F32))
                  for j in range(2)]
        emit_ffn_wprep(nc, c, block, "f1", WFB, f1wfb, f1wfss)
        if stage == "full":
            @block.sync
            def _(sy):
                for ci in range(NCH):
                    c.wmark(sy, ("f1aW", ci))
                    c.R["wfdS"].issue(sy, s_wf16[:, ci * D:(ci + 1) * D],
                                      f1wfb[:, ci * D:(ci + 1) * D])

        side = None
        if do_p6:
            CPT = max(1, NCH // NT)
            wof6 = [f1w_es.enter_context(nc.sbuf_tensor(f"wof6{j}", [128, D], F32))
                    for j in range(2)]
            whib = [f1w_es.enter_context(nc.sbuf_tensor(f"whib{j}", [128, D], BF16))
                    for j in range(2)]
            wlob = [f1w_es.enter_context(nc.sbuf_tensor(f"wlob{j}", [128, D], BF16))
                    for j in range(2)]

            def _w6_store(sy, ci):
                c.wmark(sy, ("w6vL", ci))
                c.R["w6dS"].issue(sy, s_wohi[:, ci * D:(ci + 1) * D],
                                  whib[ci % 2][:, :])
                c.R["w6dS"].issue(sy, s_wolo[:, ci * D:(ci + 1) * D],
                                  wlob[ci % 2][:, :])

            def _w6_sync(sy, i):
                for ci in range(i * CPT, min((i + 1) * CPT, NCH)):
                    if ci >= 2:
                        c.wmark(sy, ("w6vL", ci - 2))
                    c.R["w6dW"].issue(sy, wof6[ci % 2][:, :], WOB[:, ci, :])
                for ci in range((i - 1) * CPT, min(i * CPT, NCH)):
                    if ci >= 0:
                        _w6_store(sy, ci)

            def _w6_scalar(s, i):
                for ci in range(i * CPT, min((i + 1) * CPT, NCH)):
                    c.R["w6dW"].wait(s, ci)
                    if ci >= 2:
                        c.R["w6dS"].wait(s, 2 * (ci - 2))
                    s.activation(whib[ci % 2][:, :], wof6[ci % 2][:, :],
                                 AF.Identity)
                    c.mark(("w6aH", ci), "seqA")

            def _w6_vector(v, i):
                for ci in range(i * CPT, min((i + 1) * CPT, NCH)):
                    c.wmark(v, ("w6aH", ci))
                    if ci >= 2:
                        c.R["w6dS"].wait(v, 2 * (ci - 2) + 1)
                    v.tensor_tensor(wlob[ci % 2][:, :], wof6[ci % 2][:, :],
                                    whib[ci % 2][:, :], ALU.subtract)
                    c.mark(("w6vL", ci), "seqV")

            side = {"sync": _w6_sync, "scalar": _w6_scalar, "vector": _w6_vector}

        with contextlib.ExitStack() as es2:
            p1x0 = es2.enter_context(nc.sbuf_tensor("p1x0", [128, D], F32))
            p1x1 = es2.enter_context(nc.sbuf_tensor("p1x1", [128, D], F32))
            p1x2 = es2.enter_context(nc.sbuf_tensor("p1x2", [128, D], F32))
            p1xc = es2.enter_context(nc.sbuf_tensor("p1xc", [128, D], F32))
            xts = [p1x0, p1x1, p1x2]

            @block.sync
            def _(sy):
                c.wait_barrier(sy)
                for i in range(NT):
                    if i >= 3:
                        c.wmark(sy, ("p1xc", i - 3))
                    R["p1dX"].issue(sy, xts[i % 3][:, :],
                                    X[i * 128:(i + 1) * 128, :])

            @block.vector
            def _(v):
                v = c.V(v)
                c.wait_barrier(v)
                for i in range(NT):
                    R["p1dX"].wait(v, i)
                    xt = xts[i % 3]
                    for j in range(4):
                        v.bn_stats(bnb[:, 6 * j:6 * j + 6],
                                   xt[:, 512 * j:512 * (j + 1)])
                    v.bn_aggr(mv1[:, 2 * i:2 * i + 2], bnb[:, :])
                    v.tensor_scalar(nm1[:, i:i + 1], mv1[:, 2 * i:2 * i + 1],
                                    -1.0, None, ALU.mult)
                    c.mark(("p1nm", i), "seqV")
                    v.tensor_scalar(p1xc[:, :], xt[:, :], nm1[:, i:i + 1], None,
                                    ALU.add)
                    c.mark(("p1xc", i), "seqV")
                    v.tensor_reduce(am1[:, i:i + 1], p1xc[:, :], AX.X, ALU.max,
                                    apply_absolute_value=True)

            @block.vector
            def _(v):
                v = c.V(v)
                v.tensor_scalar(rs1[:, :], mv1[:, 1:2 * NT:2], EPS, None, ALU.add)
                c.mark("ve1", "seqV")
                c.wmark(v, "sq1")
                v.reciprocal(rs1[:, :], rs1[:, :])
                v.tensor_tensor(am1[:, :], am1[:, :], rs1[:, :], ALU.mult)
                v.tensor_reduce(accA[:, :], am1[:, :], AX.X, ALU.max)
                c.mark("gmx0", "seqV")

            @block.scalar
            def _(s):
                s = c.A(s)
                c.wmark(s, "ve1")
                s.activation(rs1[:, :], rs1[:, :], AF.Sqrt)
                c.mark("sq1", "seqA")

            @block.tensor
            def _(t):
                c.wmark(t, "gmx0")
                c.inc(t.transpose(pF[0:1, 0:128], accA[:, :], eye32[:, :]), "tT")
            t_tr1 = c.tok("tT")

            @block.vector
            def _(v):
                v = c.V(v)
                c.wait(v, t_tr1)
                v.tensor_reduce(sc11[:, 1:2], pF[0:1, 0:128], AX.X, ALU.max)
                c.mark("gmx1", "seqV")

            @block.gpsimd
            def _(g):
                g = c.G(g)
                c.wmark(g, "gmx1")
                c.inc(g.dma_start(out=cmax_i.ap(), in_=sc11[:, 1:2]), "dG", 16)
                g.wait_ge(sems["dG"], c.cnt["dG"])
                c.inc(g.collective_compute("AllReduce", ALU.max, replica_groups=ALL8,
                                           ins=[cmax_i.ap().opt()],
                                           outs=[cmax_o.ap().opt()]), "cc")
                g.wait_ge(sems["cc"], c.cnt["cc"])
                c.inc(g.dma_start(out=sc12[:, 0:1], in_=cmax_o.ap()), "dG", 16)
                g.wait_ge(sems["dG"], c.cnt["dG"])
                g.sem_inc(sems["gC"])
                c.cnt["gC"] += 1
            t_gm1b = c.tok("gC")

            @block.vector
            def _(v):
                v = c.V(v)
                c.wait(v, t_gm1b)
                v.tensor_scalar(sc12[:, 1:2], sc12[:, 0:1], 1.0 / FP8_MAX,
                                SCALE_EPS, ALU.mult, ALU.max)
                v.tensor_scalar(sc12[:, 1:2], sc12[:, 1:2], 2.0, None, ALU.mult)
                v.reciprocal(sc12[:, 1:2], sc12[:, 1:2])
                c.mark("i2s1", "seqV")

            @block.tensor
            def _(t):
                c.wmark(t, "i2s1")
                c.inc(t.matmul(pF[:, 0:1], onerow[:, :], sc12[:, 1:2], start=True,
                               stop=True), "tT")
            t_i2s1b = c.tok("tT")

            @block.vector
            def _(v):
                v = c.V(v)
                c.wait(v, t_i2s1b)
                v.tensor_copy(accB[:, :], pF[:, 0:1])
                v.tensor_scalar(qa1[:, :], rs1[:, :], accB[:, :], None, ALU.mult)
                v.tensor_tensor(qb1[:, :], nm1[:, :], qa1[:, :], ALU.mult)
            c.snap_barrier()

        emit_ffn(nc, c, block, "f1", X, s_x1, WFB, qa1, qb1, mv2, BIA, WS, onerow,
                 pE, bnb, PSY, pT8a, pT8b, e8f, dims, ln_stats=True,
                 extra_out=(Y if stage == "ffn1" else None),
                 pre=(f1wfb, f1wfss), side=side)
        if do_p6:
            @block.sync
            def _(sy):
                for ci in range((NT - 1) * CPT, NCH):
                    _w6_store(sy, ci)
        c.snap_barrier()
        f1w_es.close()

        if stage == "ffn1":
            emit_tail(nc, c, block)
            PASS_CNTS.append(dict(c.cnt))
            return nc

        @block.vector
        def _(v):
            v = c.V(v)
            c.wait_barrier(v)
            v.tensor_scalar(rs2[:, :], mv2[:, 1:2 * NT:2], EPS, None, ALU.add)
            c.mark("ve2", "seqV")
            c.wmark(v, "sq2")
            v.reciprocal(rs2[:, :], rs2[:, :])
            v.tensor_scalar(nm2[:, :], mv2[:, 0:2 * NT:2], -1.0, None, ALU.mult)
            v.tensor_tensor(nm2[:, :], nm2[:, :], rs2[:, :], ALU.mult)

        @block.scalar
        def _(s):
            s = c.A(s)
            c.wait_barrier(s)
            c.wmark(s, "ve2")
            s.activation(rs2[:, :], rs2[:, :], AF.Sqrt)
            c.mark("sq2", "seqA")
        c.snap_barrier()

        with contextlib.ExitStack() as es2:
            xlnhi = es2.enter_context(nc.sbuf_tensor("xlnhi", [128, NCH * TH], BF16))
            xlnlo = es2.enter_context(nc.sbuf_tensor("xlnlo", [128, NCH * TH], BF16))
            x1t0 = es2.enter_context(nc.sbuf_tensor("x1t0", [128, D], F32))
            x1t1 = x1t0
            xlf = es2.enter_context(nc.sbuf_tensor("xlf", [128, D], F32))
            wst0 = es2.enter_context(nc.sbuf_tensor("wst0", [128, D], F32))
            wst1 = wst0
            whi = {}
            wlo = {}
            for p in ("k", "v", "r"):
                whi[p] = [es2.enter_context(
                    nc.sbuf_tensor(f"whi{p}{j}", [128, D], BF16)) for j in range(2)]
                wlo[p] = [es2.enter_context(
                    nc.sbuf_tensor(f"wlo{p}{j}", [128, D], BF16)) for j in range(2)]
            ksb = es2.enter_context(nc.sbuf_tensor("ksb", [128, 512], F32))
            rsg0 = es2.enter_context(nc.sbuf_tensor("rsg0", [128, 512], F32))
            rsg1 = es2.enter_context(nc.sbuf_tensor("rsg1", [128, 512], F32))
            ivc = es2.enter_context(nc.sbuf_tensor("ivc", [128, 512], F32))
            kvb = es2.enter_context(nc.sbuf_tensor("kvb", [128, 512], F32))
            scl0 = es2.enter_context(nc.sbuf_tensor("scl0", [128, TH], F32))
            scl1 = es2.enter_context(nc.sbuf_tensor("scl1", [128, TH], F32))
            stt0 = es2.enter_context(nc.sbuf_tensor("stt0", [128, TH], F32))
            stt1 = es2.enter_context(nc.sbuf_tensor("stt1", [128, TH], F32))

            sti = [es2.enter_context(nc.sbuf_tensor(f"sti{j}", [128, 512], F32))
                   for j in range(4)]
            rsi = [es2.enter_context(nc.sbuf_tensor(f"rsi{j}", [128, 512], F32))
                   for j in range(4)]
            sct = [es2.enter_context(nc.sbuf_tensor(f"sct{j}", [128, 512], F32))
                   for j in range(2)]
            obs = [es2.enter_context(nc.sbuf_tensor(f"ob{j}", [128, 512], F32))
                   for j in range(2)]
            osq = es2.enter_context(nc.sbuf_tensor("osq", [128, 512], F32))
            ccs = es2.enter_context(nc.sbuf_tensor("ccs", [128, 16], F32))
            ones_t = es2.enter_context(nc.sbuf_tensor("ones_t", [128, TH], F32))
            pos_t = es2.enter_context(nc.sbuf_tensor("pos_t", [128, TL], F32))

            @block.gpsimd
            def _(g):
                g = c.G(g)
                c.wait_barrier(g)
                g.memset(ones_t[:, :], 1.0)
                g.iota(pos_t[:, :], pattern=[[1, TL]], base=0,
                       channel_multiplier=0,
                       allow_small_or_imprecise_dtypes=True)
                c.mark("iota", "seqG")

            @block.vector
            def _(v):
                v = c.V(v)
                c.wait_barrier(v)
                c.wmark(v, "iota")
                v.tensor_scalar(pos_t[:, :], pos_t[:, :], hbase[:, :], None,
                                ALU.add)
            p5 = dict(sti=sti, rsi=rsi, sct=sct, obs=obs, osq=osq, ccs=ccs,
                      s_out=s_out, accA=accA, accB=accB, gnsum=gnsum, gnsq=gnsq,
                      carry_eff=carry_eff, ccar_i=ccar_i, ccar_o=ccar_o,
                      iseven=iseven, isodd=isodd,
                      extra_out=(Y if stage == "mixer" else None))
            for h in range(2):
                emit_p3_half(nc, c, block, h, s_x1, [x1t0, x1t1], xlf, xlnhi, xlnlo,
                             rs2, nm2, eye32, pA, pB, dims)
                emit_p4_half(nc, c, block, h, {"k": WKB, "v": WVB, "r": WRB},
                             [wst0, wst1], whi, wlo, xlnhi, xlnlo, ksb,
                             [rsg0, rsg1], ivc, kvb, [scl0, scl1], [stt0, stt1],
                             ones_t, pos_t, ld_fm, carry_loc, carry_tot,
                             s_state, s_rsig, pA, pB, pC, pD, pE, pF, dims,
                             p5=(p5 if h == 1 else None))
            t_p4mm = c.tok("p4tM")

        if do_p6:
            es6 = contextlib.ExitStack()
            wohi = es6.enter_context(nc.sbuf_tensor("wohi", [128, NCH * D], BF16))
            wolo = es6.enter_context(nc.sbuf_tensor("wolo", [128, NCH * D], BF16))

            early_free = 2 * NCH * TH * 2 + 3 * D * 4 + 12 * D * 2
            @block.gpsimd
            def _(g):
                c.wait(g, t_p4mm)
                if 2 * NCH * D * 2 > early_free:
                    c.wmark(g, ("p5vO", NCH * TC - 1))
                    for rb in ("p4dR", "p4dP", "p5dLa", "p5dLb", "p5dS"):
                        c.R[rb].wait_all(g)
                c.R["w6dS"].wait_all(g)
                for ci in range(NCH):
                    c.R["p6dW"].issue(g, wohi[:, ci * D:(ci + 1) * D],
                                      s_wohi[:, ci * D:(ci + 1) * D])
                    c.R["p6dV"].issue(g, wolo[:, ci * D:(ci + 1) * D],
                                      s_wolo[:, ci * D:(ci + 1) * D])
        c.snap_barrier()

        if stage == "mixer":
            emit_tail(nc, c, block)
            PASS_CNTS.append(dict(c.cnt))
            return nc

        @block.vector
        def _(v):
            v = c.V(v)
            c.wait_barrier(v)
            v.tensor_reduce(accA[:, :], gnsum[:, :], AX.X, ALU.add)
            v.tensor_reduce(accB[:, :], gnsq[:, :], AX.X, ALU.add)
            c.mark("gs1", "seqV")
            if resolve:
                c.wait(v, marks["gs2"])
            v.tensor_copy(sc11[:, 0:2], pE[0:1, 0:2])
            c.mark("gs3", "seqV")
            if resolve:
                c.wait(v, marks["gs4"])
            v.tensor_scalar(sc12[:, 0:2], sc12[:, 0:2], 1.0 / (D * 2 * TL), None,
                            ALU.mult)
            v.tensor_tensor(sc11[:, 0:1], sc12[:, 0:1], sc12[:, 0:1], ALU.mult)
            v.tensor_tensor(sc11[:, 1:2], sc12[:, 1:2], sc11[:, 0:1], ALU.subtract)
            v.tensor_scalar(sc11[:, 1:2], sc11[:, 1:2], EPS, None, ALU.add)
            c.mark("gve", "seqV")
            c.wmark(v, "gsq")
            v.reciprocal(sc11[:, 1:2], sc11[:, 1:2])
            v.tensor_scalar(sc12[:, 0:1], sc12[:, 0:1], -1.0, None, ALU.mult)
            c.mark("gf", "seqV")
            if resolve:
                c.wait(v, marks["gb"])
            v.tensor_copy(col1[:, :], pE[:, 0:1])
            v.tensor_copy(col2[:, :], pF[:, 0:1])
            v.tensor_scalar(afm[:, :], gnw_fm[:, :], col1[:, :], None, ALU.mult)
            v.tensor_scalar(bfm[:, :], afm[:, :], col2[:, :], None, ALU.mult)
            v.tensor_tensor(bfm[:, :], bfm[:, :], gnb_fm[:, :], ALU.add)
            v.tensor_copy(bfmh[:, :], bfm[:, :])
            v.tensor_tensor(bfml[:, :], bfm[:, :], bfmh[:, :], ALU.subtract)
            c.mark("gab", "seqV")

        @block.tensor
        def _(t):
            c.wmark(t, "gs1")
            c.inc(t.matmul(pE[0:1, 0:1], accA[:, :], onecol[:, :], start=True,
                           stop=True), "tT")
            c.inc(t.matmul(pE[0:1, 1:2], accB[:, :], onecol[:, :], start=True,
                           stop=True), "tT")
            c.mark("gs2", "tT")
            c.wmark(t, "gf")
            c.inc(t.matmul(pE[:, 0:1], onerow[:, :], sc11[:, 1:2], start=True,
                           stop=True), "tT")
            c.inc(t.matmul(pF[:, 0:1], onerow[:, :], sc12[:, 0:1], start=True,
                           stop=True), "tT")
            c.mark("gb", "tT")

        @block.gpsimd
        def _(g):
            g = c.G(g)
            c.wmark(g, "gs3")
            c.inc(g.dma_start(out=cstat_i.ap(), in_=sc11[:, 0:2]), "dG", 16)
            g.wait_ge(sems["dG"], c.cnt["dG"])
            c.inc(g.collective_compute("AllReduce", ALU.add, replica_groups=PAIRS,
                                       ins=[cstat_i.ap().opt()],
                                       outs=[cstat_o.ap().opt()]), "cc")
            g.wait_ge(sems["cc"], c.cnt["cc"])
            c.inc(g.dma_start(out=sc12[:, 0:2], in_=cstat_o.ap()), "dG", 16)
            g.wait_ge(sems["dG"], c.cnt["dG"])
            g.sem_inc(sems["gC"])
            c.cnt["gC"] += 1
            c.mark("gs4", "gC")

        @block.scalar
        def _(s):
            s = c.A(s)
            c.wait_barrier(s)
            c.wmark(s, "gve")
            s.activation(sc11[:, 1:2], sc11[:, 1:2], AF.Sqrt)
            c.mark("gsq", "seqA")
        c.snap_barrier()

        with contextlib.ExitStack() as es2:
            ofs = [es2.enter_context(nc.sbuf_tensor(f"of{j}", [128, D], F32))
                   for j in range(2)]
            ofh = [es2.enter_context(nc.sbuf_tensor(f"ofh{j}", [128, D], BF16))
                   for j in range(2)]
            ofl = [es2.enter_context(nc.sbuf_tensor(f"ofl{j}", [128, D], BF16))
                   for j in range(2)]
            x1rs = [es2.enter_context(nc.sbuf_tensor(f"x1r{j}", [128, D], F32))
                    for j in range(2)]
            x2ts = [es2.enter_context(nc.sbuf_tensor(f"x2t{j}", [128, D], F32))
                    for j in range(2)]
            ybrow = es2.enter_context(nc.sbuf_tensor("ybrow", [1, D], F32))
            emit_p6(nc, c, block, s_out, s_x1, s_x2, wohi, wolo,
                    ofs, ofh, ofl, x1rs, x2ts, afm, bfmh, bfml, ybrow,
                    mv3, nm3, am3, bnb, onerow, pA, pB, pC, pD, pE, pF,
                    dims, extra_out=(Y if stage == "wo" else None))
        es6.close()
        c.snap_barrier()

        if stage == "wo":
            emit_tail(nc, c, block)
            PASS_CNTS.append(dict(c.cnt))
            return nc

        @block.vector
        def _(v):
            v = c.V(v)
            c.wait_barrier(v)
            v.tensor_scalar(rs3[:, :], mv3[:, 1:2 * NT:2], EPS, None, ALU.add)
            c.mark("ve3", "seqV")
            c.wmark(v, "sq3")
            v.reciprocal(rs3[:, :], rs3[:, :])
            v.tensor_tensor(am3[:, :], am3[:, :], rs3[:, :], ALU.mult)
            v.tensor_reduce(accA[:, :], am3[:, :], AX.X, ALU.max)
            c.mark("gmx3", "seqV")
            if resolve:
                c.wait(v, marks["tr3"])
            v.tensor_reduce(sc11[:, 1:2], pF[0:1, 0:128], AX.X, ALU.max)
            c.mark("gm3", "seqV")
            if resolve:
                c.wait(v, marks["gm3b"])
            v.tensor_scalar(sc12[:, 1:2], sc12[:, 0:1], 1.0 / FP8_MAX, SCALE_EPS,
                            ALU.mult, ALU.max)
            v.tensor_scalar(sc12[:, 1:2], sc12[:, 1:2], 2.0, None, ALU.mult)
            v.reciprocal(sc12[:, 1:2], sc12[:, 1:2])
            c.mark("i2s3", "seqV")
            if resolve:
                c.wait(v, marks["i2s3b"])
            v.tensor_copy(accB[:, :], pF[:, 0:1])
            v.tensor_scalar(qa3[:, :], rs3[:, :], accB[:, :], None, ALU.mult)
            v.tensor_tensor(qb3[:, :], nm3[:, :], rs3[:, :], ALU.mult)
            v.tensor_scalar(qb3[:, :], qb3[:, :], accB[:, :], None, ALU.mult)

        @block.scalar
        def _(s):
            s = c.A(s)
            c.wait_barrier(s)
            c.wmark(s, "ve3")
            s.activation(rs3[:, :], rs3[:, :], AF.Sqrt)
            c.mark("sq3", "seqA")

        @block.tensor
        def _(t):
            c.wmark(t, "gmx3")
            c.inc(t.transpose(pF[0:1, 0:128], accA[:, :], eye32[:, :]), "tT")
            c.mark("tr3", "tT")
            c.wmark(t, "i2s3")
            c.inc(t.matmul(pF[:, 0:1], onerow[:, :], sc12[:, 1:2], start=True,
                           stop=True), "tT")
            c.mark("i2s3b", "tT")

        @block.gpsimd
        def _(g):
            g = c.G(g)
            c.wmark(g, "gm3")
            c.inc(g.dma_start(out=cmax2_i.ap(), in_=sc11[:, 1:2]), "dG", 16)
            g.wait_ge(sems["dG"], c.cnt["dG"])
            c.inc(g.collective_compute("AllReduce", ALU.max, replica_groups=ALL8,
                                       ins=[cmax2_i.ap().opt()],
                                       outs=[cmax2_o.ap().opt()]), "cc")
            g.wait_ge(sems["cc"], c.cnt["cc"])
            c.inc(g.dma_start(out=sc12[:, 0:1], in_=cmax2_o.ap()), "dG", 16)
            g.wait_ge(sems["dG"], c.cnt["dG"])
            g.sem_inc(sems["gC"])
            c.cnt["gC"] += 1
            c.mark("gm3b", "gC")
        c.snap_barrier()

        emit_ffn(nc, c, block, "f2", s_x2, Y, WFB, qa3, qb3, None, BIA, WS, onerow,
                 pE, bnb, PSY, pT8a, pT8b, e8f, dims, ln_stats=False,
                 extra_out=None, pre_dram=s_wf16)
        emit_tail(nc, c, block)
        PASS_CNTS.append(dict(c.cnt))
    return nc


def emit_tail(nc, c, block):
    @block.sync
    def _(sy):
        for base in ("dOx", "f1dS", "f2dS", "p4dP", "p4dR", "p5dS", "p6dS",
                     "w6dS"):
            c.R[base].wait_all(sy)

    @block.gpsimd
    def _(g):
        if c.cnt["dG"] > 0:
            g.wait_ge(c.sems["dG"], c.cnt["dG"])


def emit_ffn_wprep(nc, c, block, px, WFB, wfb, wfss):
    S = lambda n: px + n
    RW = c.R[px + "dW"]

    @block.sync
    def _(sy):
        c.wait_barrier(sy)
        for ci in range(NCH):
            if ci >= 2:
                c.wmark(sy, (S("aW"), ci - 2))
            RW.issue(sy, wfss[ci % 2][:, :], WFB[:, ci, :])

    @block.scalar
    def _(s):
        s = c.A(s)
        c.wait_barrier(s)
        for ci in range(NCH):
            RW.wait(s, ci)
            s.activation(wfb[:, ci * D:(ci + 1) * D], wfss[ci % 2][:, :],
                         AF.Identity)
            c.mark((S("aW"), ci), "seqA")


def emit_ffn(nc, c, block, px, XIN, XOUT, WFB, qa, qb, mv_next, BIA, WS, onerow,
             pE, bnb, PSY, pT8a, pT8b, e8, dims, ln_stats, extra_out, pre=None,
             side=None, pre_dram=None):
    sems = c.sems
    NT = dims["NT"]
    S = lambda n: px + n
    RW, RX, RS = c.R[px + "dW"], c.R[px + "dX"], c.R[px + "dS"]
    with contextlib.ExitStack() as es2:
        brow = es2.enter_context(nc.sbuf_tensor(px + "brow", [1, D], F32))
        wsr = es2.enter_context(nc.sbuf_tensor(px + "wsr", [1, 1], F32))
        bias_b = es2.enter_context(nc.sbuf_tensor(px + "biasb", [128, D], F32))
        ws_col = es2.enter_context(nc.sbuf_tensor(px + "wscol", [128, 1], F32))
        if pre is not None:
            wfb, wfss = pre
        else:
            wfb = es2.enter_context(nc.sbuf_tensor(px + "wf", [128, NCH * D], BF16))
            wfss = ([es2.enter_context(
                nc.sbuf_tensor(px + f"wfs{j}", [128, D], F32)) for j in range(2)]
                if pre_dram is None else None)
        xts = [es2.enter_context(nc.sbuf_tensor(px + f"xt{j}", [128, D], F32))
               for j in range(5)]
        q8 = es2.enter_context(nc.sbuf_tensor(px + "q8", [128, D], FP8))
        qTs = [es2.enter_context(nc.sbuf_tensor(px + f"qT{j}", [128, D], BF16))
               for j in range(2)]
        outs = [es2.enter_context(nc.sbuf_tensor(px + f"o{j}", [128, D], F32))
                for j in range(2)]

        @block.sync
        def _(sy):
            c.wait_barrier(sy)
            c.inc(sy.dma_start(out=brow[:, :], in_=BIA), "dC", 16)
            c.inc(sy.dma_start(out=wsr[:, :], in_=WS), "dC", 16)
            c.mark((S("bcl"),), "dC")
            if pre_dram is not None:
                c.R["wfdS"].wait_all(sy)
                for ci in range(NCH):
                    if ci >= RW.n:
                        RW.wait(sy, ci - RW.n)
                    RW.issue(sy, wfb[:, ci * D:(ci + 1) * D],
                             pre_dram[:, ci * D:(ci + 1) * D])
            elif pre is None:
                for ci in range(NCH):
                    if ci >= 2:
                        c.wmark(sy, (S("aW"), ci - 2))
                    RW.issue(sy, wfss[ci % 2][:, :], WFB[:, ci, :])
            for i in range(NT + 3):
                if side is not None and i < NT:
                    side["sync"](sy, i)
                if i < NT:
                    if i >= 5:
                        c.wmark(sy, (S("vO"), i - 5))
                    RX.issue(sy, xts[i % 5][:, :], XIN[i * 128:(i + 1) * 128, :])
                if i >= 3:
                    j = i - 3
                    c.wmark(sy, (S("vO"), j))
                    RS.issue(sy, XOUT[j * 128:(j + 1) * 128, :], outs[j % 2][:, :])
                    if extra_out is not None:
                        c.R["dOx"].issue(sy, extra_out[j * 128:(j + 1) * 128, :],
                                         outs[j % 2][:, :])

        @block.scalar
        def _(s):
            s = c.A(s)
            c.wait_barrier(s)
            if pre is None and pre_dram is None:
                for ci in range(NCH):
                    RW.wait(s, ci)
                    s.activation(wfb[:, ci * D:(ci + 1) * D], wfss[ci % 2][:, :],
                                 AF.Identity)
                    c.mark((S("aW"), ci), "seqA")
            for i in range(NT):
                RX.wait(s, i)
                if i >= 1:
                    s.wait_ge(sems[S("tP")], i * NCH)
                s.activation(q8[:, :], xts[i % 5][:, :], AF.Identity,
                             bias=qb[:, i:i + 1], scale=qa[:, i:i + 1])
                c.mark((S("aQ"), i), "seqA")
                if side is not None:
                    side["scalar"](s, i)

        @block.tensor
        def _(t):
            c.wait_barrier(t)
            c.wmark(t, (S("bcl"),))
            for e in range(NE):
                c.inc(t.matmul(PSY[e][:, :], onerow[:, :],
                               brow[:, e * 512:(e + 1) * 512], start=True, stop=True),
                      "tT")
            c.inc(t.matmul(pE[:, 0:1], onerow[:, :], wsr[:, :], start=True,
                           stop=True), "tT")
            c.mark((S("bcm"),), "tT")
            c.wmark(t, (S("bcv"),))
            if pre_dram is None:
                c.wmark(t, (S("aW"), NCH - 1))
            for i in range(NT + 1):
                if i < NT:
                    c.wmark(t, (S("aQ"), i))
                    for half in range(2):
                        if i >= 1:
                            c.wmark(t, (S("vE"), 2 * (i - 1) + half))
                        pp = pT8a if half == 0 else pT8b
                        for k in range(8):
                            ci = half * 8 + k
                            c.inc(t.transpose(
                                pp[:, k * 256:k * 256 + 256:2],
                                q8[:, ci * 128:(ci + 1) * 128], e8[:, :]), S("tP"))
                if i >= 1:
                    j = i - 1
                    for hf in range(2):
                        if j >= 1:
                            c.wmark(t, (S("vOh"), 2 * (j - 1) + hf))
                        for ci in range(NCH):
                            if pre_dram is not None and j == 0 and hf == 0:
                                RW.wait(t, ci)
                            if hf == 0 and ci == 0:
                                c.wmark(t, (S("vE"), 2 * j))
                            if hf == 0 and ci == 8:
                                c.wmark(t, (S("vE"), 2 * j + 1))
                            lhs = qTs[j % 2][:, ci * 128:(ci + 1) * 128]
                            for e2 in range(2):
                                e = hf * 2 + e2
                                mm = t.matmul(PSY[e][:, :], lhs,
                                              wfb[:, ci * D + e * 512: ci * D + (e + 1) * 512],
                                              start=(ci == 0), stop=(ci == NCH - 1))
                                if ci == NCH - 1:
                                    c.inc(mm, S("tM"))

        @block.vector
        def _(v):
            v = c.V(v)
            c.wait_barrier(v)
            if c.resolve:
                c.wait(v, c.marks[(S("bcm"),)])
            for e in range(NE):
                v.tensor_scalar(bias_b[:, e * 512:(e + 1) * 512], PSY[e][:, :],
                                0.5, None, ALU.mult)
            v.tensor_copy(ws_col[:, :], pE[:, 0:1])
            c.mark((S("bcv"),), "seqV")
            for i in range(NT + 2):
                if i < NT:
                    v.wait_ge(sems[S("tP")], i * NCH + 8)
                    if i >= 2:
                        v.wait_ge(sems[S("tM")], (i - 1) * NE)
                    v.tensor_copy(qTs[i % 2][:, 0:1024], pT8a[:, 0:2048:2])
                    c.mark((S("vE"), 2 * i), "seqV")
                    v.wait_ge(sems[S("tP")], i * NCH + 16)
                    v.tensor_copy(qTs[i % 2][:, 1024:2048], pT8b[:, 0:2048:2])
                    c.mark((S("vE"), 2 * i + 1), "seqV")
                    if side is not None:
                        side["vector"](v, i)
                if i >= 2:
                    j = i - 2
                    RS.wait(v, j - 2)
                    if extra_out is not None:
                        c.R["dOx"].wait(v, j - 2)
                    RX.wait(v, j)
                    ot = outs[j % 2]
                    for hf in range(2):
                        v.wait_ge(sems[S("tM")], j * NE + hf * 2 + 2)
                        for e2 in range(2):
                            e = hf * 2 + e2
                            sl = slice(e * 512, (e + 1) * 512)
                            v.scalar_tensor_tensor(ot[:, sl], PSY[e][:, :],
                                                   ws_col[:, :], bias_b[:, sl],
                                                   ALU.mult, ALU.add)
                            v.tensor_tensor(ot[:, sl], ot[:, sl],
                                            xts[j % 5][:, sl], ALU.add)
                        c.mark((S("vOh"), 2 * j + hf), "seqV")
                        if ln_stats:
                            for jj in (hf * 2, hf * 2 + 1):
                                v.bn_stats(bnb[:, 6 * jj:6 * jj + 6],
                                           ot[:, 512 * jj:512 * (jj + 1)])
                    c.mark((S("vO"), j), "seqV")
                    if ln_stats:
                        v.bn_aggr(mv_next[:, 2 * j:2 * j + 2], bnb[:, :])




def emit_p3_half(nc, c, block, h, s_x1, x1ts, xlf, xlnhi, xlnlo, rs2, nm2, eye32,
                 pA, pB, dims):
    sems = c.sems
    NTH, TH, THC = dims["NTH"], dims["TH"], dims["THC"]
    base = h * NTH

    @block.sync
    def _(sy):
        if h == 0:
            c.wait_barrier(sy)
        for i in range(NTH):
            gi = base + i
            if gi >= 1:
                c.wmark(sy, ("p3aL", gi - 1))
            c.R["p3dX"].issue(sy, x1ts[gi % 2][:, :],
                              s_x1[gi * 128:(gi + 1) * 128, :])

    def grp_dst(xln, i, grp):
        return xln[:, :].rearrange("p (c t) -> p c t", t=TH)[
            :, grp * 4:(grp + 1) * 4, i * 128:(i + 1) * 128]

    @block.scalar
    def _(s):
        s = c.A(s)
        if h == 0:
            c.wait_barrier(s)
        for i in range(NTH + 1):
            if i >= 1:
                gi = base + i - 1
                for grp in range(4):
                    m = gi * 4 + grp
                    s.wait_ge(sems["p3tP"], gi * NCH + (grp + 1) * 4)
                    pp = pA if grp % 2 == 0 else pB
                    s.activation(grp_dst(xlnhi, i - 1, grp), pp[:, 0:512],
                                 AF.Identity)
                    c.mark(("p3aH", m), "seqA")
            if i < NTH:
                gi = base + i
                c.R["p3dX"].wait(s, gi)
                if gi >= 1:
                    s.wait_ge(sems["p3tP"], gi * NCH)
                s.activation(xlf[:, :], x1ts[gi % 2][:, :], AF.Identity,
                             bias=nm2[:, gi:gi + 1], scale=rs2[:, gi:gi + 1])
                c.mark(("p3aL", gi), "seqA")

    @block.tensor
    def _(t):
        if h == 0:
            c.wait_barrier(t)
        else:
            c.wmark(t, ("p4aK", NCH * THC - 1))
        for i in range(NTH):
            gi = base + i
            c.wmark(t, ("p3aL", gi))
            for ci in range(NCH):
                grp = ci // 4
                m = gi * 4 + grp
                pp = pA if grp % 2 == 0 else pB
                if ci % 4 == 0 and m >= 2:
                    c.wmark(t, ("p3vL", m - 2))
                c.inc(t.transpose(pp[:, (ci % 4) * 128:(ci % 4 + 1) * 128],
                                  xlf[:, ci * 128:(ci + 1) * 128],
                                  eye32[:, :]), "p3tP")

    @block.vector
    def _(v):
        v = c.V(v)
        if h == 0:
            c.wait_barrier(v)
        for i in range(NTH):
            gi = base + i
            for grp in range(4):
                m = gi * 4 + grp
                c.wmark(v, ("p3aH", m))
                pp = pA if grp % 2 == 0 else pB
                v.tensor_tensor(grp_dst(xlnlo, i, grp), pp[:, 0:512],
                                grp_dst(xlnhi, i, grp), ALU.subtract)
                c.mark(("p3vL", m), "seqV")


def emit_p4_half(nc, c, block, h, WB, wsts, whi, wlo, xlnhi, xlnlo, ksb, rsgs,
                 ivc, kvb, scls, stts, ones_t, pos_t, ld_fm, carry_loc, carry_tot,
                 s_state, s_rsig, pA, pB, pC, pD, pE, pF, dims, p5=None):
    sems = c.sems
    TH, THC, NTH, TC = dims["TH"], dims["THC"], dims["NTH"], dims["TC"]
    projs = ["k", "v", "r"]
    banks = {"k": (pA, pB), "v": (pC, pD), "r": (pE, pF)}
    G0 = h * NCH

    def p5_dp_idx(b, tc):
        return b if tc < THC else NCH + b

    def p5_dr_idx(b, tc):
        return b * THC + tc if tc < THC else (NCH + b) * THC + (tc - THC)

    def p5_loads(sy, b):
        for tc in range(TC):
            n = b * TC + tc
            if n >= 4:
                c.wmark(sy, ("p5vO", n - 4))
            c.R["p4dP"].wait(sy, p5_dp_idx(b, tc))
            c.R["p4dR"].wait(sy, p5_dr_idx(b, tc))
            rows = slice(b * 128, (b + 1) * 128)
            sl = slice(tc * 512, (tc + 1) * 512)
            c.R["p5dLa"].issue(sy, p5["sti"][n % 4][:, :], s_state[rows, sl])
            c.R["p5dLb"].issue(sy, p5["rsi"][n % 4][:, :], s_rsig[rows, sl])

    def p5_stores(sy, b):
        for tc in range(TC):
            n = b * TC + tc
            c.wmark(sy, ("p5vO", n))
            c.R["p5dS"].issue(sy, p5["s_out"][b, :, tc * 512:(tc + 1) * 512],
                              p5["obs"][n % 2][:, :])
            if p5["extra_out"] is not None:
                c.R["dOx"].issue(
                    sy, p5["extra_out"][b * 128:(b + 1) * 128,
                                        tc * 512:(tc + 1) * 512],
                    p5["obs"][n % 2][:, :])

    def p5_scalar(s, b):
        for tc in range(TC):
            n = b * TC + tc
            c.wmark(s, ("p5vA", n))
            s.activation(p5["sct"][n % 2][:, :], p5["sct"][n % 2][:, :], AF.Exp)
            c.mark(("p5aE", n), "seqA")

    def p5_vector(v, b):
        if c.resolve:
            c.wait(v, c.marks[("car", b)])
        v.tensor_scalar(p5["carry_eff"][:, b:b + 1], p5["carry_eff"][:, b:b + 1],
                        p5["isodd"][:, :], None, ALU.mult)
        for tc in range(TC):
            n = b * TC + tc
            if n >= 2:
                c.wmark(v, ("p5aE", n - 2))
            v.tensor_scalar(p5["sct"][n % 2][:, :],
                            pos_t[:, tc * 512:(tc + 1) * 512],
                            ld_fm[:, b:b + 1], None, ALU.mult)
            c.mark(("p5vA", n), "seqV")
            c.wmark(v, ("p5aE", n))
            c.R["p5dLa"].wait(v, n)
            c.R["p5dLb"].wait(v, n)
            c.R["p5dS"].wait(v, n - 2)
            if p5["extra_out"] is not None:
                c.R["dOx"].wait(v, n - 2)
            k = n % 2
            k4 = n % 4
            v.scalar_tensor_tensor(p5["obs"][k][:, :], p5["sti"][k4][:, :],
                                   p5["carry_eff"][:, b:b + 1], p5["sct"][k][:, :],
                                   ALU.add, ALU.mult)
            v.scalar_tensor_tensor(p5["obs"][k][:, :], p5["obs"][k][:, :], 1.0,
                                   p5["rsi"][k4][:, :], ALU.mult, ALU.mult,
                                   accum_out=p5["accA"][:, :])
            v.tensor_tensor(p5["gnsum"][:, b:b + 1], p5["gnsum"][:, b:b + 1],
                            p5["accA"][:, :], ALU.add)
            v.scalar_tensor_tensor(p5["osq"][:, :], p5["obs"][k][:, :], 1.0,
                                   p5["obs"][k][:, :], ALU.mult, ALU.mult,
                                   accum_out=p5["accB"][:, :])
            c.mark(("p5vO", n), "seqV")
            v.tensor_tensor(p5["gnsq"][:, b:b + 1], p5["gnsq"][:, b:b + 1],
                            p5["accB"][:, :], ALU.add)

    def emit_spill(sy, g):
        blk = g - G0
        c.wmark(sy, ("p4vN", g))
        c.R["p4dP"].issue(
            sy, s_state[blk * 128:(blk + 1) * 128, h * TH:(h + 1) * TH],
            stts[g % 2][:, :])
        for tc in range(THC):
            nn = g * THC + tc
            c.wmark(sy, ("p4vR", nn))
            c.R["p4dR"].issue(
                sy, s_rsig[blk * 128:(blk + 1) * 128,
                           h * TH + tc * 512: h * TH + (tc + 1) * 512],
                rsgs[nn % 2][:, :])

    @block.sync
    def _(sy):
        for blk in range(NCH + 4):
            if blk < NCH:
                g = G0 + blk
                for pi in range(3):
                    nw = g * 3 + pi
                    if nw >= 1:
                        c.wmark(sy, ("p4vL", nw - 1))
                    c.R["p4dW"].issue(sy, wsts[nw % 2][:, :], WB[projs[pi]][blk])
            if 1 <= blk <= NCH:
                emit_spill(sy, G0 + blk - 1)
            if p5 is not None and 3 <= blk < NCH + 3:
                p5_stores(sy, blk - 3)
            if p5 is not None and 2 <= blk < NCH + 2:
                p5_loads(sy, blk - 2)

    def emit_hi(s, g, pi):
        nw = g * 3 + pi
        c.R["p4dW"].wait(s, nw)
        if g >= 2:
            s.wait_ge(sems["p4tM"], ((g - 2) * 3 + pi + 1) * THC)
        s.activation(whi[projs[pi]][g % 2][:, :], wsts[nw % 2][:, :], AF.Identity)
        c.mark(("p4aH", nw), "seqA")

    def emit_lo(v, g, pi):
        nw = g * 3 + pi
        c.wmark(v, ("p4aH", nw))
        v.tensor_tensor(wlo[projs[pi]][g % 2][:, :], wsts[nw % 2][:, :],
                        whi[projs[pi]][g % 2][:, :], ALU.subtract)
        c.mark(("p4vL", nw), "seqV")

    @block.scalar
    def _(s):
        s = c.A(s)
        if h == 0:
            c.wait_barrier(s)
        for pi in range(3):
            emit_hi(s, G0, pi)
        for blk in range(NCH + 3):
            if blk < NCH:
                g = G0 + blk
                if blk < NCH - 1:
                    for pi in range(3):
                        emit_hi(s, g + 1, pi)
                for tc in range(THC):
                    nn = g * THC + tc
                    s.wait_ge(sems["p4tM"], (g * 3 + 1) * THC)
                    if nn >= 1:
                        c.wmark(s, ("p4vK", nn - 1))
                    s.copy(ksb[:, :], banks["k"][tc][:, :])
                    c.mark(("p4aK", nn), "seqA")
                    s.wait_ge(sems["p4tM"], (g * 3 + 3) * THC)
                    if nn >= 2:
                        c.R["p4dR"].wait(s, nn - 2)
                    s.activation(rsgs[nn % 2][:, :], banks["r"][tc][:, :],
                                 AF.Tanh, scale=0.5)
                    c.mark(("p4aT", nn), "seqA")
                    c.wmark(s, ("p4vC", nn))
                    s.activation(ivc[:, :], ivc[:, :], AF.Exp, scale=-1.0)
                    c.mark(("p4aE", nn), "seqA")
            if p5 is not None and 2 <= blk < NCH + 2:
                p5_scalar(s, blk - 2)

    @block.tensor
    def _(t):
        if h == 0:
            c.wait_barrier(t)
        c.wmark(t, ("p3vL", (h + 1) * NTH * 4 - 1))
        for blk in range(NCH):
            g = G0 + blk
            for pi, p in enumerate(projs):
                c.wmark(t, ("p4vL", g * 3 + pi))
                bk = banks[p]
                if g >= 1:
                    if p == "k":
                        c.wmark(t, ("p4aK", g * THC - 1))
                    elif p == "v":
                        c.wmark(t, ("p4vK", g * THC - 1))
                    else:
                        c.wmark(t, ("p4aT", g * THC - 1))
                hi_s, lo_s = whi[p][g % 2], wlo[p][g % 2]
                for ci in range(NCH):
                    lh = hi_s[:, ci * 128:(ci + 1) * 128]
                    ll = lo_s[:, ci * 128:(ci + 1) * 128]
                    for tc in range(THC):
                        xh = xlnhi[:, ci * TH + tc * 512: ci * TH + (tc + 1) * 512]
                        xl = xlnlo[:, ci * TH + tc * 512: ci * TH + (tc + 1) * 512]
                        t.matmul(bk[tc][:, :], lh, xh, start=(ci == 0), stop=False)
                        t.matmul(bk[tc][:, :], lh, xl, start=False, stop=False)
                        mm = t.matmul(bk[tc][:, :], ll, xh, start=False,
                                      stop=(ci == NCH - 1))
                        if ci == NCH - 1:
                            c.inc(mm, "p4tM")

    @block.vector
    def _(v):
        v = c.V(v)
        if h == 0:
            c.wait_barrier(v)
        for pi in range(3):
            emit_lo(v, G0, pi)
        for blk in range(NCH + 3):
            if blk < NCH:
                g = G0 + blk
                if blk < NCH - 1:
                    for pi in range(3):
                        emit_lo(v, g + 1, pi)
                for tc in range(THC):
                    nn = g * THC + tc
                    v.tensor_scalar(
                        ivc[:, :],
                        pos_t[:, h * TH + tc * 512: h * TH + (tc + 1) * 512],
                        ld_fm[:, blk:blk + 1], -23.025850929940457,
                        ALU.mult, ALU.max)
                    c.mark(("p4vC", nn), "seqV")
                    c.wmark(v, ("p4aT", nn))
                    v.tensor_scalar(rsgs[nn % 2][:, :], rsgs[nn % 2][:, :],
                                    0.5, 0.5, ALU.mult, ALU.add)
                    c.mark(("p4vR", nn), "seqV")
                    c.wmark(v, ("p4aE", nn))
                    v.tensor_scalar(ivc[:, :], ivc[:, :], 1e10, None, ALU.min)
                    c.wmark(v, ("p4aK", nn))
                    v.wait_ge(sems["p4tM"], (g * 3 + 2) * THC)
                    v.tensor_tensor(kvb[:, :], ksb[:, :], banks["v"][tc][:, :],
                                    ALU.mult)
                    c.mark(("p4vK", nn), "seqV")
                    v.tensor_tensor(scls[g % 2][:, tc * 512:(tc + 1) * 512],
                                    kvb[:, :], ivc[:, :], ALU.mult)
                if g >= 2:
                    c.R["p4dP"].wait(v, g - 2)
                init = 0.0 if h == 0 else carry_loc[:, blk:blk + 1]
                v.tensor_tensor_scan(stts[g % 2][:, :], ones_t[:, :],
                                     scls[g % 2][:, :], init, ALU.mult, ALU.add)
                dstc = carry_loc if h == 0 else carry_tot
                v.tensor_copy(dstc[:, blk:blk + 1], stts[g % 2][:, TH - 1:TH])
                c.mark(("p4vN", g), "seqV")
                if p5 is not None:
                    v.tensor_scalar(p5["ccs"][:, blk:blk + 1],
                                    carry_tot[:, blk:blk + 1],
                                    p5["iseven"][:, :], None, ALU.mult)
                    c.mark(("ctm", blk), "seqV")
            if p5 is not None and 2 <= blk < NCH + 2:
                p5_vector(v, blk - 2)

    if p5 is not None:
        @block.gpsimd
        def _(g):
            g = c.G(g)
            for q in range(NCH // 2):
                bs = range(2 * q, 2 * q + 2)
                c.wmark(g, ("ctm", 2 * q + 1))
                for b in bs:
                    c.inc(g.dma_start(out=p5["ccar_i"].ap()[b],
                                      in_=p5["ccs"][:, b:b + 1]), "dG", 16)
                g.wait_ge(sems["dG"], c.cnt["dG"])
                c.inc(g.collective_compute(
                    "AllReduce", ALU.add, replica_groups=PAIRS,
                    ins=[p5["ccar_i"].ap()[2 * q:2 * q + 2].opt()],
                    outs=[p5["ccar_o"].ap()[2 * q:2 * q + 2].opt()]), "cc")
                g.wait_ge(sems["cc"], c.cnt["cc"])
                for b in bs:
                    c.inc(g.dma_start(out=p5["carry_eff"][:, b:b + 1],
                                      in_=p5["ccar_o"].ap()[b]), "dG", 16)
                g.wait_ge(sems["dG"], c.cnt["dG"])
                g.sem_inc(sems["gC"])
                c.cnt["gC"] += 1
                for b in bs:
                    c.mark(("car", b), "gC")


def emit_p6(nc, c, block, s_out, s_x1, s_x2, wohi, wolo, ofs, ofh, ofl,
            x1rs, x2ts, afm, bfmh, bfml, ybrow, mv3, nm3, am3, bnb, onerow,
            pA, pB, pC, pD, pE, pF, dims, extra_out):
    sems = c.sems
    NT = dims["NT"]
    PSY = (pA, pB, pC, pD)
    PYB = (pC, pD, pE, pF)

    @block.sync
    def _(sy):
        c.wait_barrier(sy)
        for i in range(NT + 2):
            if i < NT:
                if i >= 2:
                    c.wmark(sy, ("p6vF", i - 2))
                c.R["p6dF"].issue(
                    sy, ofs[i % 2][:, :].rearrange("p (c j) -> p c j", j=128),
                    s_out[:, :, i * 128:(i + 1) * 128].rearrange("c p j -> p c j"))
                if i >= 2:
                    c.wmark(sy, ("p6vS", i - 2))
                c.R["p6dX"].issue(sy, x1rs[i % 2][:, :],
                                  s_x1[i * 128:(i + 1) * 128, :])
            if i >= 2:
                j = i - 2
                c.wmark(sy, ("p6vO", j))
                c.R["p6dS"].issue(sy, s_x2[j * 128:(j + 1) * 128, :],
                                  x2ts[j % 2][:, :])
                if extra_out is not None:
                    c.R["dOx"].issue(sy, extra_out[j * 128:(j + 1) * 128, :],
                                     x2ts[j % 2][:, :])

    @block.tensor
    def _(t):
        c.wait_barrier(t)
        c.wmark(t, "gab")
        c.R["p6dW"].wait_all(t)
        c.R["p6dV"].wait_all(t)
        for ci in range(NCH):
            for e in range(NE):
                sl = slice(ci * D + e * 512, ci * D + (e + 1) * 512)
                t.matmul(PYB[e][0:1, 0:512], bfmh[:, ci:ci + 1], wohi[:, sl],
                         start=(ci == 0), stop=False)
                t.matmul(PYB[e][0:1, 0:512], bfmh[:, ci:ci + 1], wolo[:, sl],
                         start=False, stop=False)
                mm = t.matmul(PYB[e][0:1, 0:512], bfml[:, ci:ci + 1], wohi[:, sl],
                              start=False, stop=(ci == NCH - 1))
                if ci == NCH - 1:
                    c.inc(mm, "p6tY")
        c.wmark(t, "p6ybr")
        for i in range(NT):
            c.wmark(t, ("p6vF", i))
            for hf in range(2):
                if i >= 1:
                    c.wmark(t, ("p6vOh", 2 * (i - 1) + hf))
                for ci in range(NCH):
                    lh = ofh[i % 2][:, ci * 128:(ci + 1) * 128]
                    ll = ofl[i % 2][:, ci * 128:(ci + 1) * 128]
                    for e2 in range(2):
                        e = hf * 2 + e2
                        sl = slice(ci * D + e * 512, ci * D + (e + 1) * 512)
                        t.matmul(PSY[e][:, :], lh, wohi[:, sl], start=(ci == 0),
                                 stop=False)
                        t.matmul(PSY[e][:, :], lh, wolo[:, sl], start=False,
                                 stop=False)
                        t.matmul(PSY[e][:, :], ll, wohi[:, sl], start=False,
                                 stop=False)
                        if ci == NCH - 1:
                            mm = t.matmul(PSY[e][:, :], onerow[:, :],
                                          ybrow[:, e * 512:(e + 1) * 512],
                                          start=False, stop=True)
                            c.inc(mm, "p6tM")

    @block.scalar
    def _(s):
        s = c.A(s)
        c.wait_barrier(s)
        c.wmark(s, "gab")
        for i in range(NT):
            c.R["p6dF"].wait(s, i)
            if i >= 2:
                s.wait_ge(sems["p6tM"], (i - 1) * NE)
            for cc in range(NCH):
                sl = slice(cc * 128, (cc + 1) * 128)
                s.activation(ofh[i % 2][:, sl], ofs[i % 2][:, sl], AF.Identity,
                             scale=afm[:, cc:cc + 1])
            c.mark(("p6aC", i), "seqA")

    @block.vector
    def _(v):
        v = c.V(v)
        c.wait_barrier(v)
        v.wait_ge(sems["p6tY"], NE)
        for e in range(NE):
            v.tensor_copy(ybrow[:, e * 512:(e + 1) * 512], PYB[e][0:1, 0:512])
        c.mark("p6ybr", "seqV")
        for i in range(NT + 2):
            if i < NT:
                c.wmark(v, ("p6aC", i))
                for cc in range(NCH):
                    sl = slice(cc * 128, (cc + 1) * 128)
                    v.scalar_tensor_tensor(ofl[i % 2][:, sl], ofs[i % 2][:, sl],
                                           afm[:, cc:cc + 1], ofh[i % 2][:, sl],
                                           ALU.mult, ALU.subtract)
                c.mark(("p6vF", i), "seqV")
            if 1 <= i <= NT:
                j = i - 1
                if j >= 2:
                    c.R["p6dS"].wait(v, j - 2)
                    if extra_out is not None:
                        c.R["dOx"].wait(v, j - 2)
                c.R["p6dX"].wait(v, j)
                for hf in range(2):
                    v.wait_ge(sems["p6tM"], j * NE + hf * 2 + 2)
                    for e2 in range(2):
                        e = hf * 2 + e2
                        sl = slice(e * 512, (e + 1) * 512)
                        v.tensor_tensor(x2ts[j % 2][:, sl], PSY[e][:, :],
                                        x1rs[j % 2][:, sl], ALU.add)
                    c.mark(("p6vOh", 2 * j + hf), "seqV")
                for jj in range(4):
                    v.bn_stats(bnb[:, 6 * jj:6 * jj + 6],
                               x2ts[j % 2][:, 512 * jj:512 * (jj + 1)])
                v.bn_aggr(mv3[:, 2 * j:2 * j + 2], bnb[:, :])
                v.tensor_scalar(nm3[:, j:j + 1], mv3[:, 2 * j:2 * j + 1], -1.0,
                                None, ALU.mult)
                v.tensor_scalar(x1rs[j % 2][:, :], x2ts[j % 2][:, :],
                                nm3[:, j:j + 1], None, ALU.add)
                v.tensor_reduce(am3[:, j:j + 1], x1rs[j % 2][:, :], AX.X, ALU.max,
                                apply_absolute_value=True)
                c.mark(("p6vO", j), "seqV")
                c.mark(("p6vS", j), "seqV")



def _host_prep(inputs, TL):
    x = np.ascontiguousarray(np.asarray(inputs["L0_attn_x"], dtype=np.float32))
    Wk = np.asarray(inputs["L0_attn_Wk"], dtype=np.float32)
    Wv = np.asarray(inputs["L0_attn_Wv"], dtype=np.float32)
    Wr = np.asarray(inputs["L0_attn_Wr"], dtype=np.float32)
    Wo = np.asarray(inputs["L0_attn_Wo"], dtype=np.float32)
    dec = np.ascontiguousarray(np.asarray(inputs["L0_attn_decay"],
                                          dtype=np.float32).reshape(1, D))
    gnw = np.ascontiguousarray(np.asarray(inputs["L0_attn_gn_weight"],
                                          dtype=np.float32).reshape(16, 128))
    gnb = np.ascontiguousarray(np.asarray(inputs["L0_attn_gn_bias"],
                                          dtype=np.float32).reshape(16, 128))
    Wf = np.asarray(inputs["L0_ffn_weight"], dtype=np.float32)
    ws = np.ascontiguousarray(np.asarray(inputs["L0_ffn_weight_scale"],
                                         dtype=np.float32).reshape(1, 1))
    bia = np.ascontiguousarray(np.asarray(inputs["L0_ffn_bias"],
                                          dtype=np.float32).reshape(1, D))

    def blkT(W):
        WT = np.ascontiguousarray(W.T)
        return np.ascontiguousarray(
            WT.reshape(NCH, 128, NCH, 128).transpose(2, 1, 0, 3).reshape(
                NCH, 128, NCH * 128))

    def rowT(W):
        WT = np.ascontiguousarray(W.T)
        return np.ascontiguousarray(WT.reshape(NCH, 128, D).transpose(1, 0, 2))

    wkb, wvb, wrb = blkT(Wk), blkT(Wv), blkT(Wr)
    wob, wfb = rowT(Wo), rowT(Wf)
    eye32 = np.eye(128, dtype=np.float32)
    eye16 = np.eye(128, dtype=ml_dtypes.bfloat16)
    onerow = np.ones((1, 128), np.float32)
    onecol = np.ones((128, 1), np.float32)

    in_maps = []
    for core in range(8):
        b, h = core // 2, core % 2
        in_maps.append({
            "x": np.ascontiguousarray(x[b, h * TL:(h + 1) * TL, :]),
            "wkb": wkb, "wvb": wvb, "wrb": wrb, "wob": wob, "wfb": wfb,
            "dec": dec, "gnw": gnw, "gnb": gnb, "bia": bia, "ws": ws,
            "hbase": np.full((128, 1), float(h * TL), np.float32),
            "isodd": np.full((128, 1), float(h == 1), np.float32),
            "iseven": np.full((128, 1), float(h == 0), np.float32),
            "eye32": eye32, "eye16": eye16, "onerow": onerow, "onecol": onecol,
        })
    return in_maps


def kernel(**inputs):
    TL = 2048
    in_maps = _host_prep(inputs, TL)
    nc = build_nc(stage="full", TL=TL)
    res = run_bass_kernel_spmd(nc, in_maps, core_ids=list(range(8)))
    out = np.empty((4, 2 * TL, D), dtype=np.float32)
    for core in range(8):
        b, h = core // 2, core % 2
        out[b, h * TL:(h + 1) * TL, :] = res.results[core]["y"]
    return out


# version 14
# speedup vs baseline: 1.0081x; 1.0081x over previous
import contextlib
import numpy as np
import ml_dtypes
import concourse.bass as bass
import concourse.mybir as mybir
from concourse.bass_utils import run_bass_kernel_spmd

F32 = mybir.dt.float32
BF16 = mybir.dt.bfloat16
FP8 = mybir.dt.float8e4
AF = mybir.ActivationFunctionType
ALU = mybir.AluOpType
AX = mybir.AxisListType

D = 2048
NCH = 16
NE = 4
EPS = 1e-5
FP8_MAX = 448.0
SCALE_EPS = 1e-7

ALL8 = [list(range(8))]
PAIRS = [[0, 1], [2, 3], [4, 5], [6, 7]]

def _ring_names(base, n):
    return [f"{base}_{j}" for j in range(n)]


RING_SPECS = {
    "p1dX": 3,
    "f1dW": 2, "f1dX": 5, "f1dS": 3,
    "f2dW": 6, "f2dX": 5, "f2dS": 3,
    "p3dX": 2,
    "p4dW": 2, "p4dP": 2, "p4dR": 2,
    "p5dLa": 4, "p5dLb": 4, "p5dS": 2,
    "p6dW": 4, "p6dV": 4, "p6dF": 2, "p6dX": 2, "p6dS": 2,
    "w6dW": 2, "w6dS": 4, "wfdS": 2, "dOx": 2,
}

SEM_NAMES = (
    ["seqV", "seqA", "seqG", "gC", "tT", "cc", "dG", "dC", "dO"]
    + [p + s for p in ("f1", "f2") for s in ("tP", "tM")]
    + ["p3tP", "p4tM", "p6tY", "p6tM"]
    + [n for base, k in RING_SPECS.items() for n in _ring_names(base, k)]
)


class DmaRing:

    def __init__(self, c, base, nslots, inc_per=16):
        self.c = c
        self.base = base
        self.n = nslots
        self.inc_per = inc_per
        self.issued = 0

    def sem(self, i):
        return f"{self.base}_{i % self.n}"

    def issue(self, builder, out, in_, **kw):
        i = self.issued
        self.c.inc(builder.dma_start(out=out, in_=in_, **kw), self.sem(i),
                   self.inc_per)
        self.issued += 1
        return i

    def wait(self, eng, i):
        if i < 0:
            return
        eng.wait_ge(self.c.sems[self.sem(i)], self.inc_per * (i // self.n + 1))

    def wait_all(self, eng):
        for j in range(self.n):
            tot = self.inc_per * ((self.issued - 1 - j) // self.n + 1) \
                if self.issued > j else 0
            if tot > 0:
                eng.wait_ge(self.c.sems[f"{self.base}_{j}"], tot)

_CHAIN_EXCLUDE = {"wait_ge", "dma_start", "dma_start_transpose", "collective_compute",
                  "indirect_dma_start", "nop", "engine", "bass", "sem_inc", "sem_clear"}


class _Chained:
    def __init__(self, c, eng, sem):
        object.__setattr__(self, "_c", c)
        object.__setattr__(self, "_e", eng)
        object.__setattr__(self, "_s", sem)

    def __getattr__(self, name):
        f = getattr(self._e, name)
        if name in _CHAIN_EXCLUDE or not callable(f):
            return f
        c, e, s = self._c, self._e, self._s

        def g(*a, **kw):
            if c.cnt[s] > 0:
                e.wait_ge(c.sems[s], c.cnt[s])
            instr = f(*a, **kw)
            c.inc(instr, s, 1)
            return instr

        return g


class Ctx:
    def __init__(self, nc, sems, marks, resolve):
        self.nc = nc
        self.sems = sems
        self.cnt = {k: 0 for k in sems}
        self.marks = marks
        self.resolve = resolve
        self._bar = []

    def inc(self, instr, name, n=1):
        instr.then_inc(self.sems[name], n)
        self.cnt[name] += n
        return (name, self.cnt[name])

    def tok(self, name):
        return (name, self.cnt[name])

    def wait(self, eng, token):
        if token is not None and token[1] > 0:
            eng.wait_ge(self.sems[token[0]], token[1])

    def mark(self, key, sem):
        if not self.resolve:
            self.marks[key] = (sem, self.cnt[sem])

    def wmark(self, eng, key):
        if self.resolve:
            sem, cnt = self.marks[key]
            if cnt > 0:
                eng.wait_ge(self.sems[sem], cnt)

    def V(self, eng):
        return _Chained(self, eng, "seqV")

    def A(self, eng):
        return _Chained(self, eng, "seqA")

    def G(self, eng):
        return _Chained(self, eng, "seqG")

    def snap_barrier(self):
        self._bar = [(k, v) for k, v in self.cnt.items() if v > 0]

    def wait_barrier(self, eng):
        for k, v in self._bar:
            eng.wait_ge(self.sems[k], v)


PASS_CNTS = []


def build_nc(stage="full", TL=2048):
    marks = {}
    PASS_CNTS.clear()
    _build(stage, TL, marks, resolve=False)
    return _build(stage, TL, marks, resolve=True)


def _build(stage, TL, marks, resolve):
    NT = TL // 128
    TH = TL // 2
    NTH = NT // 2
    THC = TH // 512
    TC = TL // 512
    assert TH % 512 == 0

    nc = bass.Bass("TRN2", target_bir_lowering=False, debug=False)

    X = nc.dram_tensor("x", [TL, D], F32, kind="ExternalInput").ap()
    WKB = nc.dram_tensor("wkb", [NCH, 128, NCH * 128], F32, kind="ExternalInput").ap()
    WVB = nc.dram_tensor("wvb", [NCH, 128, NCH * 128], F32, kind="ExternalInput").ap()
    WRB = nc.dram_tensor("wrb", [NCH, 128, NCH * 128], F32, kind="ExternalInput").ap()
    WOB = nc.dram_tensor("wob", [128, NCH, D], F32, kind="ExternalInput").ap()
    WFB = nc.dram_tensor("wfb", [128, NCH, D], F32, kind="ExternalInput").ap()
    DEC = nc.dram_tensor("dec", [1, D], F32, kind="ExternalInput").ap()
    GNW = nc.dram_tensor("gnw", [16, 128], F32, kind="ExternalInput").ap()
    GNB = nc.dram_tensor("gnb", [16, 128], F32, kind="ExternalInput").ap()
    BIA = nc.dram_tensor("bia", [1, D], F32, kind="ExternalInput").ap()
    WS = nc.dram_tensor("ws", [1, 1], F32, kind="ExternalInput").ap()
    HBASE = nc.dram_tensor("hbase", [128, 1], F32, kind="ExternalInput").ap()
    ISODD = nc.dram_tensor("isodd", [128, 1], F32, kind="ExternalInput").ap()
    ISEVEN = nc.dram_tensor("iseven", [128, 1], F32, kind="ExternalInput").ap()
    EYE32 = nc.dram_tensor("eye32", [128, 128], F32, kind="ExternalInput").ap()
    EYE16 = nc.dram_tensor("eye16", [128, 128], BF16, kind="ExternalInput").ap()
    ONEROW = nc.dram_tensor("onerow", [1, 128], F32, kind="ExternalInput").ap()
    ONECOL = nc.dram_tensor("onecol", [128, 1], F32, kind="ExternalInput").ap()

    Y = nc.dram_tensor("y", [TL, D], F32, kind="ExternalOutput").ap()

    s_x1 = nc.dram_tensor("s_x1", [TL, D], F32).ap()
    s_x2 = nc.dram_tensor("s_x2", [TL, D], F32).ap()
    s_state = nc.dram_tensor("s_state", [D, TL], F32).ap()
    s_rsig = nc.dram_tensor("s_rsig", [D, TL], F32).ap()
    s_out = nc.dram_tensor("s_out", [NCH, 128, TL], F32).ap()
    s_ld = nc.dram_tensor("s_ld", [16, 128], F32).ap()
    s_wohi = nc.dram_tensor("s_wohi", [128, NCH * D], BF16).ap()
    s_wolo = nc.dram_tensor("s_wolo", [128, NCH * D], BF16).ap()
    s_wf16 = nc.dram_tensor("s_wf16", [128, NCH * D], BF16).ap()
    cmax_i = nc.dram_tensor("cmax_i", [1, 1], F32)
    cmax_o = nc.dram_tensor("cmax_o", [1, 1], F32)
    ccar_i = nc.dram_tensor("ccar_i", [16, 128], F32)
    ccar_o = nc.dram_tensor("ccar_o", [16, 128], F32)
    cstat_i = nc.dram_tensor("cstat_i", [1, 2], F32)
    cstat_o = nc.dram_tensor("cstat_o", [1, 2], F32)
    cmax2_i = nc.dram_tensor("cmax2_i", [1, 1], F32)
    cmax2_o = nc.dram_tensor("cmax2_o", [1, 1], F32)

    es = contextlib.ExitStack()
    with es:
        sems = {n: es.enter_context(nc.semaphore(n)) for n in SEM_NAMES}
        block = es.enter_context(nc.Block())
        c = Ctx(nc, sems, marks, resolve)
        R = {base: DmaRing(c, base, k) for base, k in RING_SPECS.items()}
        c.R = R
        dims = dict(NT=NT, TH=TH, NTH=NTH, THC=THC, TC=TC)

        pA = nc.alloc_psum_tensor("pA", [128, 512], F32)
        pB = nc.alloc_psum_tensor("pB", [128, 512], F32)
        pC = nc.alloc_psum_tensor("pC", [128, 512], F32)
        pD = nc.alloc_psum_tensor("pD", [128, 512], F32)
        pE = nc.alloc_psum_tensor("pE", [128, 512], F32)
        pF = nc.alloc_psum_tensor("pF", [128, 512], F32)
        pT8a = nc.alloc_psum_tensor("pT8a", [128, 2048], FP8)
        pT8b = nc.alloc_psum_tensor("pT8b", [128, 2048], FP8)
        PSY = (pA, pB, pC, pD)

        eye32 = nc.alloc_sbuf_tensor("sb_eye32", [128, 128], F32)
        eye16 = nc.alloc_sbuf_tensor("sb_eye16", [128, 128], BF16)
        e8f = nc.alloc_sbuf_tensor("sb_e8", [128, 128], FP8)
        onerow = nc.alloc_sbuf_tensor("sb_onerow", [1, 128], F32)
        onecol = nc.alloc_sbuf_tensor("sb_onecol", [128, 1], F32)
        hbase = nc.alloc_sbuf_tensor("sb_hbase", [128, 1], F32)
        isodd = nc.alloc_sbuf_tensor("sb_isodd", [128, 1], F32)
        iseven = nc.alloc_sbuf_tensor("sb_iseven", [128, 1], F32)
        ld_fm = nc.alloc_sbuf_tensor("ld_fm", [128, 16], F32)
        gnw_fm = nc.alloc_sbuf_tensor("gnw_fm", [128, 16], F32)
        gnb_fm = nc.alloc_sbuf_tensor("gnb_fm", [128, 16], F32)
        mv1 = nc.alloc_sbuf_tensor("mv1", [128, 2 * NT], F32)
        mv2 = nc.alloc_sbuf_tensor("mv2", [128, 2 * NT], F32)
        mv3 = nc.alloc_sbuf_tensor("mv3", [128, 2 * NT], F32)
        am1 = nc.alloc_sbuf_tensor("am1", [128, NT], F32)
        am3 = nc.alloc_sbuf_tensor("am3", [128, NT], F32)
        rs1 = nc.alloc_sbuf_tensor("rs1", [128, NT], F32)
        rs2 = nc.alloc_sbuf_tensor("rs2", [128, NT], F32)
        rs3 = nc.alloc_sbuf_tensor("rs3", [128, NT], F32)
        nm1 = nc.alloc_sbuf_tensor("nm1", [128, NT], F32)
        nm2 = nc.alloc_sbuf_tensor("nm2", [128, NT], F32)
        nm3 = nc.alloc_sbuf_tensor("nm3", [128, NT], F32)
        qa1 = nc.alloc_sbuf_tensor("qa1", [128, NT], F32)
        qb1 = nc.alloc_sbuf_tensor("qb1", [128, NT], F32)
        qa3 = nc.alloc_sbuf_tensor("qa3", [128, NT], F32)
        qb3 = nc.alloc_sbuf_tensor("qb3", [128, NT], F32)
        sc11 = nc.alloc_sbuf_tensor("sc11", [1, 2], F32)
        sc12 = nc.alloc_sbuf_tensor("sc12", [1, 2], F32)
        carry_loc = nc.alloc_sbuf_tensor("carry_loc", [128, 16], F32)
        carry_tot = nc.alloc_sbuf_tensor("carry_tot", [128, 16], F32)
        carry_eff = nc.alloc_sbuf_tensor("carry_eff", [128, 16], F32)
        gnsum = nc.alloc_sbuf_tensor("gnsum", [128, 16], F32)
        gnsq = nc.alloc_sbuf_tensor("gnsq", [128, 16], F32)
        afm = nc.alloc_sbuf_tensor("afm", [128, 16], F32)
        bfm = nc.alloc_sbuf_tensor("bfm", [128, 16], F32)
        bfmh = nc.alloc_sbuf_tensor("bfmh", [128, 16], BF16)
        bfml = nc.alloc_sbuf_tensor("bfml", [128, 16], BF16)
        col1 = nc.alloc_sbuf_tensor("col1", [128, 1], F32)
        col2 = nc.alloc_sbuf_tensor("col2", [128, 1], F32)
        accA = nc.alloc_sbuf_tensor("accA", [128, 1], F32)
        accB = nc.alloc_sbuf_tensor("accB", [128, 1], F32)
        bnb = nc.alloc_sbuf_tensor("bnb", [128, 24], F32)

        es0 = contextlib.ExitStack()
        vrow = es0.enter_context(nc.sbuf_tensor("vrow", [1, D], F32))
        vrow2 = es0.enter_context(nc.sbuf_tensor("vrow2", [1, D], F32))

        @block.sync
        def _(sy):
            for dst, src in [(eye32[:, :], EYE32), (eye16[:, :], EYE16),
                             (onerow[:, :], ONEROW), (onecol[:, :], ONECOL),
                             (hbase[:, :], HBASE), (isodd[:, :], ISODD),
                             (iseven[:, :], ISEVEN), (vrow[:, :], DEC)]:
                c.inc(sy.dma_start(out=dst, in_=src), "dC", 16)
        t_consts = ("dC", 8 * 16)

        @block.gpsimd
        def _(g):
            g = c.G(g)
            g.memset(gnsum[:, :], 0.0)
            g.memset(gnsq[:, :], 0.0)
            c.inc(g.dma_start(out=gnw_fm[:, :], in_=GNW.rearrange("c p -> p c"),
                              allow_slow_non_contiguous=True), "dG", 16)
            c.inc(g.dma_start(out=gnb_fm[:, :], in_=GNB.rearrange("c p -> p c"),
                              allow_slow_non_contiguous=True), "dG", 16)
            c.wmark(g, "ld0")
            c.inc(g.dma_start(out=s_ld.rearrange("c p -> (c p)")[None, :],
                              in_=vrow2[:, :]), "dG", 16)
            g.wait_ge(sems["dG"], c.cnt["dG"])
            c.inc(g.dma_start(out=ld_fm[:, :], in_=s_ld.rearrange("c p -> p c"),
                              allow_slow_non_contiguous=True), "dG", 16)

        @block.scalar
        def _(s):
            s = c.A(s)
            c.wait(s, t_consts)
            s.activation(e8f[:, :], eye32[:, :], AF.Identity)
            s.activation(vrow2[:, :], vrow[:, :], AF.Tanh, scale=0.5)
            c.mark("tanh0", "seqA")
            c.wmark(s, "sig0")
            s.activation(vrow2[:, :], vrow2[:, :], AF.Ln)
            c.mark("ld0", "seqA")

        @block.vector
        def _(v):
            v = c.V(v)
            c.wait(v, t_consts)
            c.wmark(v, "tanh0")
            v.tensor_scalar(vrow2[:, :], vrow2[:, :], 0.5, 0.5, ALU.mult, ALU.add)
            v.tensor_scalar(vrow2[:, :], vrow2[:, :], SCALE_EPS, None, ALU.max)
            c.mark("sig0", "seqV")

        c.snap_barrier()
        es0.close()

        do_p6 = stage in ("full", "wo")
        f1w_es = contextlib.ExitStack()
        f1wfb = f1w_es.enter_context(nc.sbuf_tensor("f1wf", [128, NCH * D], BF16))
        f1wfss = [f1w_es.enter_context(nc.sbuf_tensor(f"f1wfs{j}", [128, D], F32))
                  for j in range(2)]
        emit_ffn_wprep(nc, c, block, "f1", WFB, f1wfb, f1wfss)
        if stage == "full":
            @block.sync
            def _(sy):
                for ci in range(NCH):
                    c.wmark(sy, ("f1aW", ci))
                    c.R["wfdS"].issue(sy, s_wf16[:, ci * D:(ci + 1) * D],
                                      f1wfb[:, ci * D:(ci + 1) * D])

        side = None
        if do_p6:
            CPT = max(1, NCH // NT)
            wof6 = [f1w_es.enter_context(nc.sbuf_tensor(f"wof6{j}", [128, D], F32))
                    for j in range(2)]
            whib = [f1w_es.enter_context(nc.sbuf_tensor(f"whib{j}", [128, D], BF16))
                    for j in range(2)]
            wlob = [f1w_es.enter_context(nc.sbuf_tensor(f"wlob{j}", [128, D], BF16))
                    for j in range(2)]

            def _w6_store(sy, ci):
                c.wmark(sy, ("w6vL", ci))
                c.R["w6dS"].issue(sy, s_wohi[:, ci * D:(ci + 1) * D],
                                  whib[ci % 2][:, :])
                c.R["w6dS"].issue(sy, s_wolo[:, ci * D:(ci + 1) * D],
                                  wlob[ci % 2][:, :])

            def _w6_sync(sy, i):
                for ci in range(i * CPT, min((i + 1) * CPT, NCH)):
                    if ci >= 2:
                        c.wmark(sy, ("w6vL", ci - 2))
                    c.R["w6dW"].issue(sy, wof6[ci % 2][:, :], WOB[:, ci, :])
                for ci in range((i - 1) * CPT, min(i * CPT, NCH)):
                    if ci >= 0:
                        _w6_store(sy, ci)

            def _w6_scalar(s, i):
                for ci in range(i * CPT, min((i + 1) * CPT, NCH)):
                    c.R["w6dW"].wait(s, ci)
                    if ci >= 2:
                        c.R["w6dS"].wait(s, 2 * (ci - 2))
                    s.activation(whib[ci % 2][:, :], wof6[ci % 2][:, :],
                                 AF.Identity)
                    c.mark(("w6aH", ci), "seqA")

            def _w6_vector(v, i):
                for ci in range(i * CPT, min((i + 1) * CPT, NCH)):
                    c.wmark(v, ("w6aH", ci))
                    if ci >= 2:
                        c.R["w6dS"].wait(v, 2 * (ci - 2) + 1)
                    v.tensor_tensor(wlob[ci % 2][:, :], wof6[ci % 2][:, :],
                                    whib[ci % 2][:, :], ALU.subtract)
                    c.mark(("w6vL", ci), "seqV")

            side = {"sync": _w6_sync, "scalar": _w6_scalar, "vector": _w6_vector}

        with contextlib.ExitStack() as es2:
            p1x0 = es2.enter_context(nc.sbuf_tensor("p1x0", [128, D], F32))
            p1x1 = es2.enter_context(nc.sbuf_tensor("p1x1", [128, D], F32))
            p1x2 = es2.enter_context(nc.sbuf_tensor("p1x2", [128, D], F32))
            p1xc = es2.enter_context(nc.sbuf_tensor("p1xc", [128, D], F32))
            xts = [p1x0, p1x1, p1x2]

            @block.sync
            def _(sy):
                c.wait_barrier(sy)
                for i in range(NT):
                    if i >= 3:
                        c.wmark(sy, ("p1xc", i - 3))
                    R["p1dX"].issue(sy, xts[i % 3][:, :],
                                    X[i * 128:(i + 1) * 128, :])

            @block.vector
            def _(v):
                v = c.V(v)
                c.wait_barrier(v)
                for i in range(NT):
                    R["p1dX"].wait(v, i)
                    xt = xts[i % 3]
                    for j in range(4):
                        v.bn_stats(bnb[:, 6 * j:6 * j + 6],
                                   xt[:, 512 * j:512 * (j + 1)])
                    v.bn_aggr(mv1[:, 2 * i:2 * i + 2], bnb[:, :])
                    v.tensor_scalar(nm1[:, i:i + 1], mv1[:, 2 * i:2 * i + 1],
                                    -1.0, None, ALU.mult)
                    c.mark(("p1nm", i), "seqV")
                    v.tensor_scalar(p1xc[:, :], xt[:, :], nm1[:, i:i + 1], None,
                                    ALU.add)
                    c.mark(("p1xc", i), "seqV")
                    v.tensor_reduce(am1[:, i:i + 1], p1xc[:, :], AX.X, ALU.max,
                                    apply_absolute_value=True)

            @block.vector
            def _(v):
                v = c.V(v)
                v.tensor_scalar(rs1[:, :], mv1[:, 1:2 * NT:2], EPS, None, ALU.add)
                c.mark("ve1", "seqV")
                c.wmark(v, "sq1")
                v.reciprocal(rs1[:, :], rs1[:, :])
                v.tensor_tensor(am1[:, :], am1[:, :], rs1[:, :], ALU.mult)
                v.tensor_reduce(accA[:, :], am1[:, :], AX.X, ALU.max)
                c.mark("gmx0", "seqV")

            @block.scalar
            def _(s):
                s = c.A(s)
                c.wmark(s, "ve1")
                s.activation(rs1[:, :], rs1[:, :], AF.Sqrt)
                c.mark("sq1", "seqA")

            @block.tensor
            def _(t):
                c.wmark(t, "gmx0")
                c.inc(t.transpose(pF[0:1, 0:128], accA[:, :], eye32[:, :]), "tT")
            t_tr1 = c.tok("tT")

            @block.vector
            def _(v):
                v = c.V(v)
                c.wait(v, t_tr1)
                v.tensor_reduce(sc11[:, 1:2], pF[0:1, 0:128], AX.X, ALU.max)
                c.mark("gmx1", "seqV")

            @block.gpsimd
            def _(g):
                g = c.G(g)
                c.wmark(g, "gmx1")
                c.inc(g.dma_start(out=cmax_i.ap(), in_=sc11[:, 1:2]), "dG", 16)
                g.wait_ge(sems["dG"], c.cnt["dG"])
                c.inc(g.collective_compute("AllReduce", ALU.max, replica_groups=ALL8,
                                           ins=[cmax_i.ap().opt()],
                                           outs=[cmax_o.ap().opt()]), "cc")
                g.wait_ge(sems["cc"], c.cnt["cc"])
                c.inc(g.dma_start(out=sc12[:, 0:1], in_=cmax_o.ap()), "dG", 16)
                g.wait_ge(sems["dG"], c.cnt["dG"])
                g.sem_inc(sems["gC"])
                c.cnt["gC"] += 1
            t_gm1b = c.tok("gC")

            @block.vector
            def _(v):
                v = c.V(v)
                c.wait(v, t_gm1b)
                v.tensor_scalar(sc12[:, 1:2], sc12[:, 0:1], 1.0 / FP8_MAX,
                                SCALE_EPS, ALU.mult, ALU.max)
                v.tensor_scalar(sc12[:, 1:2], sc12[:, 1:2], 2.0, None, ALU.mult)
                v.reciprocal(sc12[:, 1:2], sc12[:, 1:2])
                c.mark("i2s1", "seqV")

            @block.tensor
            def _(t):
                c.wmark(t, "i2s1")
                c.inc(t.matmul(pF[:, 0:1], onerow[:, :], sc12[:, 1:2], start=True,
                               stop=True), "tT")
            t_i2s1b = c.tok("tT")

            @block.vector
            def _(v):
                v = c.V(v)
                c.wait(v, t_i2s1b)
                v.tensor_copy(accB[:, :], pF[:, 0:1])
                v.tensor_scalar(qa1[:, :], rs1[:, :], accB[:, :], None, ALU.mult)
                v.tensor_tensor(qb1[:, :], nm1[:, :], qa1[:, :], ALU.mult)
            c.snap_barrier()

        emit_ffn(nc, c, block, "f1", X, s_x1, WFB, qa1, qb1, mv2, BIA, WS, onerow,
                 pE, bnb, PSY, pT8a, pT8b, e8f, dims, ln_stats=True,
                 extra_out=(Y if stage == "ffn1" else None),
                 pre=(f1wfb, f1wfss), side=side)
        if do_p6:
            @block.sync
            def _(sy):
                for ci in range((NT - 1) * CPT, NCH):
                    _w6_store(sy, ci)
        c.snap_barrier()
        f1w_es.close()

        if stage == "ffn1":
            emit_tail(nc, c, block)
            PASS_CNTS.append(dict(c.cnt))
            return nc

        @block.vector
        def _(v):
            v = c.V(v)
            c.wait_barrier(v)
            v.tensor_scalar(rs2[:, :], mv2[:, 1:2 * NT:2], EPS, None, ALU.add)
            c.mark("ve2", "seqV")
            c.wmark(v, "sq2")
            v.reciprocal(rs2[:, :], rs2[:, :])
            v.tensor_scalar(nm2[:, :], mv2[:, 0:2 * NT:2], -1.0, None, ALU.mult)
            v.tensor_tensor(nm2[:, :], nm2[:, :], rs2[:, :], ALU.mult)

        @block.scalar
        def _(s):
            s = c.A(s)
            c.wait_barrier(s)
            c.wmark(s, "ve2")
            s.activation(rs2[:, :], rs2[:, :], AF.Sqrt)
            c.mark("sq2", "seqA")
        c.snap_barrier()

        with contextlib.ExitStack() as es2:
            xlnhi = es2.enter_context(nc.sbuf_tensor("xlnhi", [128, NCH * TH], BF16))
            xlnlo = es2.enter_context(nc.sbuf_tensor("xlnlo", [128, NCH * TH], BF16))
            x1t0 = es2.enter_context(nc.sbuf_tensor("x1t0", [128, D], F32))
            x1t1 = x1t0
            xlf = es2.enter_context(nc.sbuf_tensor("xlf", [128, D], F32))
            wst0 = es2.enter_context(nc.sbuf_tensor("wst0", [128, D], F32))
            wst1 = wst0
            whi = {}
            wlo = {}
            for p in ("k", "v", "r"):
                whi[p] = [es2.enter_context(
                    nc.sbuf_tensor(f"whi{p}{j}", [128, D], BF16)) for j in range(2)]
                wlo[p] = [es2.enter_context(
                    nc.sbuf_tensor(f"wlo{p}{j}", [128, D], BF16)) for j in range(2)]
            ksb = es2.enter_context(nc.sbuf_tensor("ksb", [128, 512], F32))
            rsg0 = es2.enter_context(nc.sbuf_tensor("rsg0", [128, 512], F32))
            rsg1 = es2.enter_context(nc.sbuf_tensor("rsg1", [128, 512], F32))
            ivc = es2.enter_context(nc.sbuf_tensor("ivc", [128, 512], F32))
            kvb = es2.enter_context(nc.sbuf_tensor("kvb", [128, 512], F32))
            scl0 = es2.enter_context(nc.sbuf_tensor("scl0", [128, TH], F32))
            scl1 = es2.enter_context(nc.sbuf_tensor("scl1", [128, TH], F32))
            stt0 = es2.enter_context(nc.sbuf_tensor("stt0", [128, TH], F32))
            stt1 = es2.enter_context(nc.sbuf_tensor("stt1", [128, TH], F32))

            sti = [es2.enter_context(nc.sbuf_tensor(f"sti{j}", [128, 512], F32))
                   for j in range(4)]
            rsi = [es2.enter_context(nc.sbuf_tensor(f"rsi{j}", [128, 512], F32))
                   for j in range(4)]
            sct = [es2.enter_context(nc.sbuf_tensor(f"sct{j}", [128, 512], F32))
                   for j in range(2)]
            obs = [es2.enter_context(nc.sbuf_tensor(f"ob{j}", [128, 512], F32))
                   for j in range(2)]
            osq = es2.enter_context(nc.sbuf_tensor("osq", [128, 512], F32))
            ccs = es2.enter_context(nc.sbuf_tensor("ccs", [128, 16], F32))
            ones_t = es2.enter_context(nc.sbuf_tensor("ones_t", [128, TH], F32))
            pos_t = es2.enter_context(nc.sbuf_tensor("pos_t", [128, TL], F32))

            @block.gpsimd
            def _(g):
                g = c.G(g)
                c.wait_barrier(g)
                g.memset(ones_t[:, :], 1.0)
                g.iota(pos_t[:, :], pattern=[[1, TL]], base=0,
                       channel_multiplier=0,
                       allow_small_or_imprecise_dtypes=True)
                c.mark("iota", "seqG")

            @block.vector
            def _(v):
                v = c.V(v)
                c.wait_barrier(v)
                c.wmark(v, "iota")
                v.tensor_scalar(pos_t[:, :], pos_t[:, :], hbase[:, :], None,
                                ALU.add)
            p5 = dict(sti=sti, rsi=rsi, sct=sct, obs=obs, osq=osq, ccs=ccs,
                      s_out=s_out, accA=accA, accB=accB, gnsum=gnsum, gnsq=gnsq,
                      carry_eff=carry_eff, ccar_i=ccar_i, ccar_o=ccar_o,
                      iseven=iseven, isodd=isodd,
                      extra_out=(Y if stage == "mixer" else None))
            for h in range(2):
                emit_p3_half(nc, c, block, h, s_x1, [x1t0, x1t1], xlf, xlnhi, xlnlo,
                             rs2, nm2, eye32, pA, pB, dims)
                emit_p4_half(nc, c, block, h, {"k": WKB, "v": WVB, "r": WRB},
                             [wst0, wst1], whi, wlo, xlnhi, xlnlo, ksb,
                             [rsg0, rsg1], ivc, kvb, [scl0, scl1], [stt0, stt1],
                             ones_t, pos_t, ld_fm, carry_loc, carry_tot,
                             s_state, s_rsig, pA, pB, pC, pD, pE, pF, dims,
                             p5=(p5 if h == 1 else None))
            t_p4mm = c.tok("p4tM")

        if do_p6:
            es6 = contextlib.ExitStack()
            wohi = es6.enter_context(nc.sbuf_tensor("wohi", [128, NCH * D], BF16))
            wolo = es6.enter_context(nc.sbuf_tensor("wolo", [128, NCH * D], BF16))

            early_free = 2 * NCH * TH * 2 + 3 * D * 4 + 12 * D * 2
            @block.gpsimd
            def _(g):
                c.wait(g, t_p4mm)
                if 2 * NCH * D * 2 > early_free:
                    c.wmark(g, ("p5vO", NCH * TC - 1))
                    for rb in ("p4dR", "p4dP", "p5dLa", "p5dLb", "p5dS"):
                        c.R[rb].wait_all(g)
                c.R["w6dS"].wait_all(g)
                for ci in range(NCH):
                    c.R["p6dW"].issue(g, wohi[:, ci * D:(ci + 1) * D],
                                      s_wohi[:, ci * D:(ci + 1) * D])
                    c.R["p6dV"].issue(g, wolo[:, ci * D:(ci + 1) * D],
                                      s_wolo[:, ci * D:(ci + 1) * D])
        c.snap_barrier()

        if stage == "mixer":
            emit_tail(nc, c, block)
            PASS_CNTS.append(dict(c.cnt))
            return nc

        @block.vector
        def _(v):
            v = c.V(v)
            c.wait_barrier(v)
            v.tensor_reduce(accA[:, :], gnsum[:, :], AX.X, ALU.add)
            v.tensor_reduce(accB[:, :], gnsq[:, :], AX.X, ALU.add)
            c.mark("gs1", "seqV")
            if resolve:
                c.wait(v, marks["gs2"])
            v.tensor_copy(sc11[:, 0:2], pE[0:1, 0:2])
            c.mark("gs3", "seqV")
            if resolve:
                c.wait(v, marks["gs4"])
            v.tensor_scalar(sc12[:, 0:2], sc12[:, 0:2], 1.0 / (D * 2 * TL), None,
                            ALU.mult)
            v.tensor_tensor(sc11[:, 0:1], sc12[:, 0:1], sc12[:, 0:1], ALU.mult)
            v.tensor_tensor(sc11[:, 1:2], sc12[:, 1:2], sc11[:, 0:1], ALU.subtract)
            v.tensor_scalar(sc11[:, 1:2], sc11[:, 1:2], EPS, None, ALU.add)
            c.mark("gve", "seqV")
            c.wmark(v, "gsq")
            v.reciprocal(sc11[:, 1:2], sc11[:, 1:2])
            v.tensor_scalar(sc12[:, 0:1], sc12[:, 0:1], -1.0, None, ALU.mult)
            c.mark("gf", "seqV")
            if resolve:
                c.wait(v, marks["gb"])
            v.tensor_copy(col1[:, :], pE[:, 0:1])
            v.tensor_copy(col2[:, :], pF[:, 0:1])
            v.tensor_scalar(afm[:, :], gnw_fm[:, :], col1[:, :], None, ALU.mult)
            v.tensor_scalar(bfm[:, :], afm[:, :], col2[:, :], None, ALU.mult)
            v.tensor_tensor(bfm[:, :], bfm[:, :], gnb_fm[:, :], ALU.add)
            v.tensor_copy(bfmh[:, :], bfm[:, :])
            v.tensor_tensor(bfml[:, :], bfm[:, :], bfmh[:, :], ALU.subtract)
            c.mark("gab", "seqV")

        @block.tensor
        def _(t):
            c.wmark(t, "gs1")
            c.inc(t.matmul(pE[0:1, 0:1], accA[:, :], onecol[:, :], start=True,
                           stop=True), "tT")
            c.inc(t.matmul(pE[0:1, 1:2], accB[:, :], onecol[:, :], start=True,
                           stop=True), "tT")
            c.mark("gs2", "tT")
            c.wmark(t, "gf")
            c.inc(t.matmul(pE[:, 0:1], onerow[:, :], sc11[:, 1:2], start=True,
                           stop=True), "tT")
            c.inc(t.matmul(pF[:, 0:1], onerow[:, :], sc12[:, 0:1], start=True,
                           stop=True), "tT")
            c.mark("gb", "tT")

        @block.gpsimd
        def _(g):
            g = c.G(g)
            c.wmark(g, "gs3")
            c.inc(g.dma_start(out=cstat_i.ap(), in_=sc11[:, 0:2]), "dG", 16)
            g.wait_ge(sems["dG"], c.cnt["dG"])
            c.inc(g.collective_compute("AllReduce", ALU.add, replica_groups=PAIRS,
                                       ins=[cstat_i.ap().opt()],
                                       outs=[cstat_o.ap().opt()]), "cc")
            g.wait_ge(sems["cc"], c.cnt["cc"])
            c.inc(g.dma_start(out=sc12[:, 0:2], in_=cstat_o.ap()), "dG", 16)
            g.wait_ge(sems["dG"], c.cnt["dG"])
            g.sem_inc(sems["gC"])
            c.cnt["gC"] += 1
            c.mark("gs4", "gC")

        @block.scalar
        def _(s):
            s = c.A(s)
            c.wait_barrier(s)
            c.wmark(s, "gve")
            s.activation(sc11[:, 1:2], sc11[:, 1:2], AF.Sqrt)
            c.mark("gsq", "seqA")
        c.snap_barrier()

        with contextlib.ExitStack() as es2:
            ofs = [es2.enter_context(nc.sbuf_tensor(f"of{j}", [128, D], F32))
                   for j in range(2)]
            ofh = [es2.enter_context(nc.sbuf_tensor(f"ofh{j}", [128, D], BF16))
                   for j in range(2)]
            ofl = [es2.enter_context(nc.sbuf_tensor(f"ofl{j}", [128, D], BF16))
                   for j in range(2)]
            x1rs = [es2.enter_context(nc.sbuf_tensor(f"x1r{j}", [128, D], F32))
                    for j in range(2)]
            x2ts = [es2.enter_context(nc.sbuf_tensor(f"x2t{j}", [128, D], F32))
                    for j in range(2)]
            ybrow = es2.enter_context(nc.sbuf_tensor("ybrow", [1, D], F32))
            emit_p6(nc, c, block, s_out, s_x1, s_x2, wohi, wolo,
                    ofs, ofh, ofl, x1rs, x2ts, afm, bfmh, bfml, ybrow,
                    mv3, nm3, am3, bnb, onerow, pA, pB, pC, pD, pE, pF,
                    dims, extra_out=(Y if stage == "wo" else None))
        es6.close()
        c.snap_barrier()

        if stage == "wo":
            emit_tail(nc, c, block)
            PASS_CNTS.append(dict(c.cnt))
            return nc

        @block.vector
        def _(v):
            v = c.V(v)
            c.wait_barrier(v)
            v.tensor_scalar(rs3[:, :], mv3[:, 1:2 * NT:2], EPS, None, ALU.add)
            c.mark("ve3", "seqV")
            c.wmark(v, "sq3")
            v.reciprocal(rs3[:, :], rs3[:, :])
            v.tensor_tensor(am3[:, :], am3[:, :], rs3[:, :], ALU.mult)
            v.tensor_reduce(accA[:, :], am3[:, :], AX.X, ALU.max)
            c.mark("gmx3", "seqV")
            if resolve:
                c.wait(v, marks["tr3"])
            v.tensor_reduce(sc11[:, 1:2], pF[0:1, 0:128], AX.X, ALU.max)
            c.mark("gm3", "seqV")
            if resolve:
                c.wait(v, marks["gm3b"])
            v.tensor_scalar(sc12[:, 1:2], sc12[:, 0:1], 1.0 / FP8_MAX, SCALE_EPS,
                            ALU.mult, ALU.max)
            v.tensor_scalar(sc12[:, 1:2], sc12[:, 1:2], 2.0, None, ALU.mult)
            v.reciprocal(sc12[:, 1:2], sc12[:, 1:2])
            c.mark("i2s3", "seqV")
            if resolve:
                c.wait(v, marks["i2s3b"])
            v.tensor_copy(accB[:, :], pF[:, 0:1])
            v.tensor_scalar(qa3[:, :], rs3[:, :], accB[:, :], None, ALU.mult)
            v.tensor_tensor(qb3[:, :], nm3[:, :], rs3[:, :], ALU.mult)
            v.tensor_scalar(qb3[:, :], qb3[:, :], accB[:, :], None, ALU.mult)

        @block.scalar
        def _(s):
            s = c.A(s)
            c.wait_barrier(s)
            c.wmark(s, "ve3")
            s.activation(rs3[:, :], rs3[:, :], AF.Sqrt)
            c.mark("sq3", "seqA")

        @block.tensor
        def _(t):
            c.wmark(t, "gmx3")
            c.inc(t.transpose(pF[0:1, 0:128], accA[:, :], eye32[:, :]), "tT")
            c.mark("tr3", "tT")
            c.wmark(t, "i2s3")
            c.inc(t.matmul(pF[:, 0:1], onerow[:, :], sc12[:, 1:2], start=True,
                           stop=True), "tT")
            c.mark("i2s3b", "tT")

        @block.gpsimd
        def _(g):
            g = c.G(g)
            c.wmark(g, "gm3")
            c.inc(g.dma_start(out=cmax2_i.ap(), in_=sc11[:, 1:2]), "dG", 16)
            g.wait_ge(sems["dG"], c.cnt["dG"])
            c.inc(g.collective_compute("AllReduce", ALU.max, replica_groups=ALL8,
                                       ins=[cmax2_i.ap().opt()],
                                       outs=[cmax2_o.ap().opt()]), "cc")
            g.wait_ge(sems["cc"], c.cnt["cc"])
            c.inc(g.dma_start(out=sc12[:, 0:1], in_=cmax2_o.ap()), "dG", 16)
            g.wait_ge(sems["dG"], c.cnt["dG"])
            g.sem_inc(sems["gC"])
            c.cnt["gC"] += 1
            c.mark("gm3b", "gC")
        c.snap_barrier()

        emit_ffn(nc, c, block, "f2", s_x2, Y, WFB, qa3, qb3, None, BIA, WS, onerow,
                 pE, bnb, PSY, pT8a, pT8b, e8f, dims, ln_stats=False,
                 extra_out=None, pre_dram=s_wf16)
        emit_tail(nc, c, block)
        PASS_CNTS.append(dict(c.cnt))
    return nc


def emit_tail(nc, c, block):
    @block.sync
    def _(sy):
        for base in ("dOx", "f1dS", "f2dS", "p4dP", "p4dR", "p5dS", "p6dS",
                     "w6dS"):
            c.R[base].wait_all(sy)

    @block.gpsimd
    def _(g):
        if c.cnt["dG"] > 0:
            g.wait_ge(c.sems["dG"], c.cnt["dG"])


def emit_ffn_wprep(nc, c, block, px, WFB, wfb, wfss):
    S = lambda n: px + n
    RW = c.R[px + "dW"]

    @block.sync
    def _(sy):
        c.wait_barrier(sy)
        for ci in range(NCH):
            if ci >= 2:
                c.wmark(sy, (S("aW"), ci - 2))
            RW.issue(sy, wfss[ci % 2][:, :], WFB[:, ci, :])

    @block.scalar
    def _(s):
        s = c.A(s)
        c.wait_barrier(s)
        for ci in range(NCH):
            RW.wait(s, ci)
            s.activation(wfb[:, ci * D:(ci + 1) * D], wfss[ci % 2][:, :],
                         AF.Identity)
            c.mark((S("aW"), ci), "seqA")


def emit_ffn(nc, c, block, px, XIN, XOUT, WFB, qa, qb, mv_next, BIA, WS, onerow,
             pE, bnb, PSY, pT8a, pT8b, e8, dims, ln_stats, extra_out, pre=None,
             side=None, pre_dram=None):
    sems = c.sems
    NT = dims["NT"]
    S = lambda n: px + n
    RW, RX, RS = c.R[px + "dW"], c.R[px + "dX"], c.R[px + "dS"]
    with contextlib.ExitStack() as es2:
        brow = es2.enter_context(nc.sbuf_tensor(px + "brow", [1, D], F32))
        wsr = es2.enter_context(nc.sbuf_tensor(px + "wsr", [1, 1], F32))
        bias_b = es2.enter_context(nc.sbuf_tensor(px + "biasb", [128, D], F32))
        ws_col = es2.enter_context(nc.sbuf_tensor(px + "wscol", [128, 1], F32))
        if pre is not None:
            wfb, wfss = pre
        else:
            wfb = es2.enter_context(nc.sbuf_tensor(px + "wf", [128, NCH * D], BF16))
            wfss = ([es2.enter_context(
                nc.sbuf_tensor(px + f"wfs{j}", [128, D], F32)) for j in range(2)]
                if pre_dram is None else None)
        xts = [es2.enter_context(nc.sbuf_tensor(px + f"xt{j}", [128, D], F32))
               for j in range(5)]
        q8 = es2.enter_context(nc.sbuf_tensor(px + "q8", [128, D], FP8))
        qTs = [es2.enter_context(nc.sbuf_tensor(px + f"qT{j}", [128, D], BF16))
               for j in range(2)]
        outs = [es2.enter_context(nc.sbuf_tensor(px + f"o{j}", [128, D], F32))
                for j in range(3)]

        @block.sync
        def _(sy):
            c.wait_barrier(sy)
            c.inc(sy.dma_start(out=brow[:, :], in_=BIA), "dC", 16)
            c.inc(sy.dma_start(out=wsr[:, :], in_=WS), "dC", 16)
            c.mark((S("bcl"),), "dC")
            if pre_dram is not None:
                c.R["wfdS"].wait_all(sy)
                for ci in range(NCH):
                    if ci >= RW.n:
                        RW.wait(sy, ci - RW.n)
                    RW.issue(sy, wfb[:, ci * D:(ci + 1) * D],
                             pre_dram[:, ci * D:(ci + 1) * D])
            elif pre is None:
                for ci in range(NCH):
                    if ci >= 2:
                        c.wmark(sy, (S("aW"), ci - 2))
                    RW.issue(sy, wfss[ci % 2][:, :], WFB[:, ci, :])
            for i in range(NT + 3):
                if side is not None and i < NT:
                    side["sync"](sy, i)
                if i < NT:
                    if i >= 5:
                        c.wmark(sy, (S("vO"), i - 5))
                    RX.issue(sy, xts[i % 5][:, :], XIN[i * 128:(i + 1) * 128, :])
                if i >= 3:
                    j = i - 3
                    c.wmark(sy, (S("vO"), j))
                    RS.issue(sy, XOUT[j * 128:(j + 1) * 128, :], outs[j % 3][:, :])
                    if extra_out is not None:
                        c.R["dOx"].issue(sy, extra_out[j * 128:(j + 1) * 128, :],
                                         outs[j % 3][:, :])

        @block.scalar
        def _(s):
            s = c.A(s)
            c.wait_barrier(s)
            if pre is None and pre_dram is None:
                for ci in range(NCH):
                    RW.wait(s, ci)
                    s.activation(wfb[:, ci * D:(ci + 1) * D], wfss[ci % 2][:, :],
                                 AF.Identity)
                    c.mark((S("aW"), ci), "seqA")
            for i in range(NT):
                RX.wait(s, i)
                if i >= 1:
                    s.wait_ge(sems[S("tP")], i * NCH)
                s.activation(q8[:, :], xts[i % 5][:, :], AF.Identity,
                             bias=qb[:, i:i + 1], scale=qa[:, i:i + 1])
                c.mark((S("aQ"), i), "seqA")
                if side is not None:
                    side["scalar"](s, i)

        @block.tensor
        def _(t):
            c.wait_barrier(t)
            c.wmark(t, (S("bcl"),))
            for e in range(NE):
                c.inc(t.matmul(PSY[e][:, :], onerow[:, :],
                               brow[:, e * 512:(e + 1) * 512], start=True, stop=True),
                      "tT")
            c.inc(t.matmul(pE[:, 0:1], onerow[:, :], wsr[:, :], start=True,
                           stop=True), "tT")
            c.mark((S("bcm"),), "tT")
            c.wmark(t, (S("bcv"),))
            if pre_dram is None:
                c.wmark(t, (S("aW"), NCH - 1))
            for i in range(NT + 1):
                if i < NT:
                    c.wmark(t, (S("aQ"), i))
                    for half in range(2):
                        if i >= 1:
                            c.wmark(t, (S("vE"), 2 * (i - 1) + half))
                        pp = pT8a if half == 0 else pT8b
                        for k in range(8):
                            ci = half * 8 + k
                            c.inc(t.transpose(
                                pp[:, k * 256:k * 256 + 256:2],
                                q8[:, ci * 128:(ci + 1) * 128], e8[:, :]), S("tP"))
                if i >= 1:
                    j = i - 1
                    for hf in range(2):
                        if j >= 1:
                            c.wmark(t, (S("vOh"), 2 * (j - 1) + hf))
                        for ci in range(NCH):
                            if pre_dram is not None and j == 0 and hf == 0:
                                RW.wait(t, ci)
                            if hf == 0 and ci == 0:
                                c.wmark(t, (S("vE"), 2 * j))
                            if hf == 0 and ci == 8:
                                c.wmark(t, (S("vE"), 2 * j + 1))
                            lhs = qTs[j % 2][:, ci * 128:(ci + 1) * 128]
                            for e2 in range(2):
                                e = hf * 2 + e2
                                mm = t.matmul(PSY[e][:, :], lhs,
                                              wfb[:, ci * D + e * 512: ci * D + (e + 1) * 512],
                                              start=(ci == 0), stop=(ci == NCH - 1))
                                if ci == NCH - 1:
                                    c.inc(mm, S("tM"))

        @block.vector
        def _(v):
            v = c.V(v)
            c.wait_barrier(v)
            if c.resolve:
                c.wait(v, c.marks[(S("bcm"),)])
            for e in range(NE):
                v.tensor_scalar(bias_b[:, e * 512:(e + 1) * 512], PSY[e][:, :],
                                0.5, None, ALU.mult)
            v.tensor_copy(ws_col[:, :], pE[:, 0:1])
            c.mark((S("bcv"),), "seqV")
            for i in range(NT + 2):
                if i < NT:
                    v.wait_ge(sems[S("tP")], i * NCH + 8)
                    if i >= 2:
                        v.wait_ge(sems[S("tM")], (i - 1) * NE)
                    v.tensor_copy(qTs[i % 2][:, 0:1024], pT8a[:, 0:2048:2])
                    c.mark((S("vE"), 2 * i), "seqV")
                    v.wait_ge(sems[S("tP")], i * NCH + 16)
                    v.tensor_copy(qTs[i % 2][:, 1024:2048], pT8b[:, 0:2048:2])
                    c.mark((S("vE"), 2 * i + 1), "seqV")
                    if side is not None:
                        side["vector"](v, i)
                if i >= 2:
                    j = i - 2
                    RS.wait(v, j - 3)
                    if extra_out is not None:
                        c.R["dOx"].wait(v, j - 2)
                    RX.wait(v, j)
                    ot = outs[j % 3]
                    for hf in range(2):
                        v.wait_ge(sems[S("tM")], j * NE + hf * 2 + 2)
                        for e2 in range(2):
                            e = hf * 2 + e2
                            sl = slice(e * 512, (e + 1) * 512)
                            v.scalar_tensor_tensor(ot[:, sl], PSY[e][:, :],
                                                   ws_col[:, :], bias_b[:, sl],
                                                   ALU.mult, ALU.add)
                            v.tensor_tensor(ot[:, sl], ot[:, sl],
                                            xts[j % 5][:, sl], ALU.add)
                        c.mark((S("vOh"), 2 * j + hf), "seqV")
                        if ln_stats:
                            for jj in (hf * 2, hf * 2 + 1):
                                v.bn_stats(bnb[:, 6 * jj:6 * jj + 6],
                                           ot[:, 512 * jj:512 * (jj + 1)])
                    c.mark((S("vO"), j), "seqV")
                    if ln_stats:
                        v.bn_aggr(mv_next[:, 2 * j:2 * j + 2], bnb[:, :])




def emit_p3_half(nc, c, block, h, s_x1, x1ts, xlf, xlnhi, xlnlo, rs2, nm2, eye32,
                 pA, pB, dims):
    sems = c.sems
    NTH, TH, THC = dims["NTH"], dims["TH"], dims["THC"]
    base = h * NTH

    @block.sync
    def _(sy):
        if h == 0:
            c.wait_barrier(sy)
        for i in range(NTH):
            gi = base + i
            if gi >= 1:
                c.wmark(sy, ("p3aL", gi - 1))
            c.R["p3dX"].issue(sy, x1ts[gi % 2][:, :],
                              s_x1[gi * 128:(gi + 1) * 128, :])

    def grp_dst(xln, i, grp):
        return xln[:, :].rearrange("p (c t) -> p c t", t=TH)[
            :, grp * 4:(grp + 1) * 4, i * 128:(i + 1) * 128]

    @block.scalar
    def _(s):
        s = c.A(s)
        if h == 0:
            c.wait_barrier(s)
        for i in range(NTH + 1):
            if i >= 1:
                gi = base + i - 1
                for grp in range(4):
                    m = gi * 4 + grp
                    s.wait_ge(sems["p3tP"], gi * NCH + (grp + 1) * 4)
                    pp = pA if grp % 2 == 0 else pB
                    s.activation(grp_dst(xlnhi, i - 1, grp), pp[:, 0:512],
                                 AF.Identity)
                    c.mark(("p3aH", m), "seqA")
            if i < NTH:
                gi = base + i
                c.R["p3dX"].wait(s, gi)
                if gi >= 1:
                    s.wait_ge(sems["p3tP"], gi * NCH)
                s.activation(xlf[:, :], x1ts[gi % 2][:, :], AF.Identity,
                             bias=nm2[:, gi:gi + 1], scale=rs2[:, gi:gi + 1])
                c.mark(("p3aL", gi), "seqA")

    @block.tensor
    def _(t):
        if h == 0:
            c.wait_barrier(t)
        else:
            c.wmark(t, ("p4aK", NCH * THC - 1))
        for i in range(NTH):
            gi = base + i
            c.wmark(t, ("p3aL", gi))
            for ci in range(NCH):
                grp = ci // 4
                m = gi * 4 + grp
                pp = pA if grp % 2 == 0 else pB
                if ci % 4 == 0 and m >= 2:
                    c.wmark(t, ("p3vL", m - 2))
                c.inc(t.transpose(pp[:, (ci % 4) * 128:(ci % 4 + 1) * 128],
                                  xlf[:, ci * 128:(ci + 1) * 128],
                                  eye32[:, :]), "p3tP")

    @block.vector
    def _(v):
        v = c.V(v)
        if h == 0:
            c.wait_barrier(v)
        for i in range(NTH):
            gi = base + i
            for grp in range(4):
                m = gi * 4 + grp
                c.wmark(v, ("p3aH", m))
                pp = pA if grp % 2 == 0 else pB
                v.tensor_tensor(grp_dst(xlnlo, i, grp), pp[:, 0:512],
                                grp_dst(xlnhi, i, grp), ALU.subtract)
                c.mark(("p3vL", m), "seqV")


def emit_p4_half(nc, c, block, h, WB, wsts, whi, wlo, xlnhi, xlnlo, ksb, rsgs,
                 ivc, kvb, scls, stts, ones_t, pos_t, ld_fm, carry_loc, carry_tot,
                 s_state, s_rsig, pA, pB, pC, pD, pE, pF, dims, p5=None):
    sems = c.sems
    TH, THC, NTH, TC = dims["TH"], dims["THC"], dims["NTH"], dims["TC"]
    projs = ["k", "v", "r"]
    banks = {"k": (pA, pB), "v": (pC, pD), "r": (pE, pF)}
    G0 = h * NCH

    def p5_dp_idx(b, tc):
        return b if tc < THC else NCH + b

    def p5_dr_idx(b, tc):
        return b * THC + tc if tc < THC else (NCH + b) * THC + (tc - THC)

    def p5_loads(sy, b):
        for tc in range(TC):
            n = b * TC + tc
            if n >= 4:
                c.wmark(sy, ("p5vO", n - 4))
            c.R["p4dP"].wait(sy, p5_dp_idx(b, tc))
            c.R["p4dR"].wait(sy, p5_dr_idx(b, tc))
            rows = slice(b * 128, (b + 1) * 128)
            sl = slice(tc * 512, (tc + 1) * 512)
            c.R["p5dLa"].issue(sy, p5["sti"][n % 4][:, :], s_state[rows, sl])
            c.R["p5dLb"].issue(sy, p5["rsi"][n % 4][:, :], s_rsig[rows, sl])

    def p5_stores(sy, b):
        for tc in range(TC):
            n = b * TC + tc
            c.wmark(sy, ("p5vO", n))
            c.R["p5dS"].issue(sy, p5["s_out"][b, :, tc * 512:(tc + 1) * 512],
                              p5["obs"][n % 2][:, :])
            if p5["extra_out"] is not None:
                c.R["dOx"].issue(
                    sy, p5["extra_out"][b * 128:(b + 1) * 128,
                                        tc * 512:(tc + 1) * 512],
                    p5["obs"][n % 2][:, :])

    def p5_scalar(s, b):
        for tc in range(TC):
            n = b * TC + tc
            c.wmark(s, ("p5vA", n))
            s.activation(p5["sct"][n % 2][:, :], p5["sct"][n % 2][:, :], AF.Exp)
            c.mark(("p5aE", n), "seqA")

    def p5_vector(v, b):
        if c.resolve:
            c.wait(v, c.marks[("car", b)])
        v.tensor_scalar(p5["carry_eff"][:, b:b + 1], p5["carry_eff"][:, b:b + 1],
                        p5["isodd"][:, :], None, ALU.mult)
        for tc in range(TC):
            n = b * TC + tc
            if n >= 2:
                c.wmark(v, ("p5aE", n - 2))
            v.tensor_scalar(p5["sct"][n % 2][:, :],
                            pos_t[:, tc * 512:(tc + 1) * 512],
                            ld_fm[:, b:b + 1], None, ALU.mult)
            c.mark(("p5vA", n), "seqV")
            c.wmark(v, ("p5aE", n))
            c.R["p5dLa"].wait(v, n)
            c.R["p5dLb"].wait(v, n)
            c.R["p5dS"].wait(v, n - 2)
            if p5["extra_out"] is not None:
                c.R["dOx"].wait(v, n - 2)
            k = n % 2
            k4 = n % 4
            v.scalar_tensor_tensor(p5["obs"][k][:, :], p5["sti"][k4][:, :],
                                   p5["carry_eff"][:, b:b + 1], p5["sct"][k][:, :],
                                   ALU.add, ALU.mult)
            v.scalar_tensor_tensor(p5["obs"][k][:, :], p5["obs"][k][:, :], 1.0,
                                   p5["rsi"][k4][:, :], ALU.mult, ALU.mult,
                                   accum_out=p5["accA"][:, :])
            v.tensor_tensor(p5["gnsum"][:, b:b + 1], p5["gnsum"][:, b:b + 1],
                            p5["accA"][:, :], ALU.add)
            v.scalar_tensor_tensor(p5["osq"][:, :], p5["obs"][k][:, :], 1.0,
                                   p5["obs"][k][:, :], ALU.mult, ALU.mult,
                                   accum_out=p5["accB"][:, :])
            c.mark(("p5vO", n), "seqV")
            v.tensor_tensor(p5["gnsq"][:, b:b + 1], p5["gnsq"][:, b:b + 1],
                            p5["accB"][:, :], ALU.add)

    def emit_spill(sy, g):
        blk = g - G0
        c.wmark(sy, ("p4vN", g))
        c.R["p4dP"].issue(
            sy, s_state[blk * 128:(blk + 1) * 128, h * TH:(h + 1) * TH],
            stts[g % 2][:, :])
        for tc in range(THC):
            nn = g * THC + tc
            c.wmark(sy, ("p4vR", nn))
            c.R["p4dR"].issue(
                sy, s_rsig[blk * 128:(blk + 1) * 128,
                           h * TH + tc * 512: h * TH + (tc + 1) * 512],
                rsgs[nn % 2][:, :])

    @block.sync
    def _(sy):
        for blk in range(NCH + 4):
            if blk < NCH:
                g = G0 + blk
                for pi in range(3):
                    nw = g * 3 + pi
                    if nw >= 1:
                        c.wmark(sy, ("p4vL", nw - 1))
                    c.R["p4dW"].issue(sy, wsts[nw % 2][:, :], WB[projs[pi]][blk])
            if 1 <= blk <= NCH:
                emit_spill(sy, G0 + blk - 1)
            if p5 is not None and 3 <= blk < NCH + 3:
                p5_stores(sy, blk - 3)
            if p5 is not None and 2 <= blk < NCH + 2:
                p5_loads(sy, blk - 2)

    def emit_hi(s, g, pi):
        nw = g * 3 + pi
        c.R["p4dW"].wait(s, nw)
        if g >= 2:
            s.wait_ge(sems["p4tM"], ((g - 2) * 3 + pi + 1) * THC)
        s.activation(whi[projs[pi]][g % 2][:, :], wsts[nw % 2][:, :], AF.Identity)
        c.mark(("p4aH", nw), "seqA")

    def emit_lo(v, g, pi):
        nw = g * 3 + pi
        c.wmark(v, ("p4aH", nw))
        v.tensor_tensor(wlo[projs[pi]][g % 2][:, :], wsts[nw % 2][:, :],
                        whi[projs[pi]][g % 2][:, :], ALU.subtract)
        c.mark(("p4vL", nw), "seqV")

    @block.scalar
    def _(s):
        s = c.A(s)
        if h == 0:
            c.wait_barrier(s)
        for pi in range(3):
            emit_hi(s, G0, pi)
        for blk in range(NCH + 3):
            if blk < NCH:
                g = G0 + blk
                if blk < NCH - 1:
                    for pi in range(3):
                        emit_hi(s, g + 1, pi)
                for tc in range(THC):
                    nn = g * THC + tc
                    s.wait_ge(sems["p4tM"], (g * 3 + 1) * THC)
                    if nn >= 1:
                        c.wmark(s, ("p4vK", nn - 1))
                    s.copy(ksb[:, :], banks["k"][tc][:, :])
                    c.mark(("p4aK", nn), "seqA")
                    s.wait_ge(sems["p4tM"], (g * 3 + 3) * THC)
                    if nn >= 2:
                        c.R["p4dR"].wait(s, nn - 2)
                    s.activation(rsgs[nn % 2][:, :], banks["r"][tc][:, :],
                                 AF.Tanh, scale=0.5)
                    c.mark(("p4aT", nn), "seqA")
                    c.wmark(s, ("p4vC", nn))
                    s.activation(ivc[:, :], ivc[:, :], AF.Exp, scale=-1.0)
                    c.mark(("p4aE", nn), "seqA")
            if p5 is not None and 2 <= blk < NCH + 2:
                p5_scalar(s, blk - 2)

    @block.tensor
    def _(t):
        if h == 0:
            c.wait_barrier(t)
        c.wmark(t, ("p3vL", (h + 1) * NTH * 4 - 1))
        for blk in range(NCH):
            g = G0 + blk
            for pi, p in enumerate(projs):
                c.wmark(t, ("p4vL", g * 3 + pi))
                bk = banks[p]
                if g >= 1:
                    if p == "k":
                        c.wmark(t, ("p4aK", g * THC - 1))
                    elif p == "v":
                        c.wmark(t, ("p4vK", g * THC - 1))
                    else:
                        c.wmark(t, ("p4aT", g * THC - 1))
                hi_s, lo_s = whi[p][g % 2], wlo[p][g % 2]
                for ci in range(NCH):
                    lh = hi_s[:, ci * 128:(ci + 1) * 128]
                    ll = lo_s[:, ci * 128:(ci + 1) * 128]
                    for tc in range(THC):
                        xh = xlnhi[:, ci * TH + tc * 512: ci * TH + (tc + 1) * 512]
                        xl = xlnlo[:, ci * TH + tc * 512: ci * TH + (tc + 1) * 512]
                        t.matmul(bk[tc][:, :], lh, xh, start=(ci == 0), stop=False)
                        t.matmul(bk[tc][:, :], lh, xl, start=False, stop=False)
                        mm = t.matmul(bk[tc][:, :], ll, xh, start=False,
                                      stop=(ci == NCH - 1))
                        if ci == NCH - 1:
                            c.inc(mm, "p4tM")

    @block.vector
    def _(v):
        v = c.V(v)
        if h == 0:
            c.wait_barrier(v)
        for pi in range(3):
            emit_lo(v, G0, pi)
        for blk in range(NCH + 3):
            if blk < NCH:
                g = G0 + blk
                if blk < NCH - 1:
                    for pi in range(3):
                        emit_lo(v, g + 1, pi)
                for tc in range(THC):
                    nn = g * THC + tc
                    v.tensor_scalar(
                        ivc[:, :],
                        pos_t[:, h * TH + tc * 512: h * TH + (tc + 1) * 512],
                        ld_fm[:, blk:blk + 1], -23.025850929940457,
                        ALU.mult, ALU.max)
                    c.mark(("p4vC", nn), "seqV")
                    c.wmark(v, ("p4aT", nn))
                    v.tensor_scalar(rsgs[nn % 2][:, :], rsgs[nn % 2][:, :],
                                    0.5, 0.5, ALU.mult, ALU.add)
                    c.mark(("p4vR", nn), "seqV")
                    c.wmark(v, ("p4aE", nn))
                    v.tensor_scalar(ivc[:, :], ivc[:, :], 1e10, None, ALU.min)
                    c.wmark(v, ("p4aK", nn))
                    v.wait_ge(sems["p4tM"], (g * 3 + 2) * THC)
                    v.tensor_tensor(kvb[:, :], ksb[:, :], banks["v"][tc][:, :],
                                    ALU.mult)
                    c.mark(("p4vK", nn), "seqV")
                    v.tensor_tensor(scls[g % 2][:, tc * 512:(tc + 1) * 512],
                                    kvb[:, :], ivc[:, :], ALU.mult)
                if g >= 2:
                    c.R["p4dP"].wait(v, g - 2)
                init = 0.0 if h == 0 else carry_loc[:, blk:blk + 1]
                v.tensor_tensor_scan(stts[g % 2][:, :], ones_t[:, :],
                                     scls[g % 2][:, :], init, ALU.mult, ALU.add)
                dstc = carry_loc if h == 0 else carry_tot
                v.tensor_copy(dstc[:, blk:blk + 1], stts[g % 2][:, TH - 1:TH])
                c.mark(("p4vN", g), "seqV")
                if p5 is not None:
                    v.tensor_scalar(p5["ccs"][:, blk:blk + 1],
                                    carry_tot[:, blk:blk + 1],
                                    p5["iseven"][:, :], None, ALU.mult)
                    c.mark(("ctm", blk), "seqV")
            if p5 is not None and 2 <= blk < NCH + 2:
                p5_vector(v, blk - 2)

    if p5 is not None:
        @block.gpsimd
        def _(g):
            g = c.G(g)
            for q in range(NCH // 2):
                bs = range(2 * q, 2 * q + 2)
                c.wmark(g, ("ctm", 2 * q + 1))
                for b in bs:
                    c.inc(g.dma_start(out=p5["ccar_i"].ap()[b],
                                      in_=p5["ccs"][:, b:b + 1]), "dG", 16)
                g.wait_ge(sems["dG"], c.cnt["dG"])
                c.inc(g.collective_compute(
                    "AllReduce", ALU.add, replica_groups=PAIRS,
                    ins=[p5["ccar_i"].ap()[2 * q:2 * q + 2].opt()],
                    outs=[p5["ccar_o"].ap()[2 * q:2 * q + 2].opt()]), "cc")
                g.wait_ge(sems["cc"], c.cnt["cc"])
                for b in bs:
                    c.inc(g.dma_start(out=p5["carry_eff"][:, b:b + 1],
                                      in_=p5["ccar_o"].ap()[b]), "dG", 16)
                g.wait_ge(sems["dG"], c.cnt["dG"])
                g.sem_inc(sems["gC"])
                c.cnt["gC"] += 1
                for b in bs:
                    c.mark(("car", b), "gC")


def emit_p6(nc, c, block, s_out, s_x1, s_x2, wohi, wolo, ofs, ofh, ofl,
            x1rs, x2ts, afm, bfmh, bfml, ybrow, mv3, nm3, am3, bnb, onerow,
            pA, pB, pC, pD, pE, pF, dims, extra_out):
    sems = c.sems
    NT = dims["NT"]
    PSY = (pA, pB, pC, pD)
    PYB = (pC, pD, pE, pF)

    @block.sync
    def _(sy):
        c.wait_barrier(sy)
        for i in range(NT + 2):
            if i < NT:
                if i >= 2:
                    c.wmark(sy, ("p6vF", i - 2))
                c.R["p6dF"].issue(
                    sy, ofs[i % 2][:, :].rearrange("p (c j) -> p c j", j=128),
                    s_out[:, :, i * 128:(i + 1) * 128].rearrange("c p j -> p c j"))
                if i >= 2:
                    c.wmark(sy, ("p6vS", i - 2))
                c.R["p6dX"].issue(sy, x1rs[i % 2][:, :],
                                  s_x1[i * 128:(i + 1) * 128, :])
            if i >= 2:
                j = i - 2
                c.wmark(sy, ("p6vO", j))
                c.R["p6dS"].issue(sy, s_x2[j * 128:(j + 1) * 128, :],
                                  x2ts[j % 2][:, :])
                if extra_out is not None:
                    c.R["dOx"].issue(sy, extra_out[j * 128:(j + 1) * 128, :],
                                     x2ts[j % 2][:, :])

    @block.tensor
    def _(t):
        c.wait_barrier(t)
        c.wmark(t, "gab")
        c.R["p6dW"].wait_all(t)
        c.R["p6dV"].wait_all(t)
        for ci in range(NCH):
            for e in range(NE):
                sl = slice(ci * D + e * 512, ci * D + (e + 1) * 512)
                t.matmul(PYB[e][0:1, 0:512], bfmh[:, ci:ci + 1], wohi[:, sl],
                         start=(ci == 0), stop=False)
                t.matmul(PYB[e][0:1, 0:512], bfmh[:, ci:ci + 1], wolo[:, sl],
                         start=False, stop=False)
                mm = t.matmul(PYB[e][0:1, 0:512], bfml[:, ci:ci + 1], wohi[:, sl],
                              start=False, stop=(ci == NCH - 1))
                if ci == NCH - 1:
                    c.inc(mm, "p6tY")
        c.wmark(t, "p6ybr")
        for i in range(NT):
            c.wmark(t, ("p6vF", i))
            for hf in range(2):
                if i >= 1:
                    c.wmark(t, ("p6vOh", 2 * (i - 1) + hf))
                for ci in range(NCH):
                    lh = ofh[i % 2][:, ci * 128:(ci + 1) * 128]
                    ll = ofl[i % 2][:, ci * 128:(ci + 1) * 128]
                    for e2 in range(2):
                        e = hf * 2 + e2
                        sl = slice(ci * D + e * 512, ci * D + (e + 1) * 512)
                        t.matmul(PSY[e][:, :], lh, wohi[:, sl], start=(ci == 0),
                                 stop=False)
                        t.matmul(PSY[e][:, :], lh, wolo[:, sl], start=False,
                                 stop=False)
                        t.matmul(PSY[e][:, :], ll, wohi[:, sl], start=False,
                                 stop=False)
                        if ci == NCH - 1:
                            mm = t.matmul(PSY[e][:, :], onerow[:, :],
                                          ybrow[:, e * 512:(e + 1) * 512],
                                          start=False, stop=True)
                            c.inc(mm, "p6tM")

    @block.scalar
    def _(s):
        s = c.A(s)
        c.wait_barrier(s)
        c.wmark(s, "gab")
        for i in range(NT):
            c.R["p6dF"].wait(s, i)
            if i >= 2:
                s.wait_ge(sems["p6tM"], (i - 1) * NE)
            for cc in range(NCH):
                sl = slice(cc * 128, (cc + 1) * 128)
                s.activation(ofh[i % 2][:, sl], ofs[i % 2][:, sl], AF.Identity,
                             scale=afm[:, cc:cc + 1])
            c.mark(("p6aC", i), "seqA")

    @block.vector
    def _(v):
        v = c.V(v)
        c.wait_barrier(v)
        v.wait_ge(sems["p6tY"], NE)
        for e in range(NE):
            v.tensor_copy(ybrow[:, e * 512:(e + 1) * 512], PYB[e][0:1, 0:512])
        c.mark("p6ybr", "seqV")
        for i in range(NT + 2):
            if i < NT:
                c.wmark(v, ("p6aC", i))
                for cc in range(NCH):
                    sl = slice(cc * 128, (cc + 1) * 128)
                    v.scalar_tensor_tensor(ofl[i % 2][:, sl], ofs[i % 2][:, sl],
                                           afm[:, cc:cc + 1], ofh[i % 2][:, sl],
                                           ALU.mult, ALU.subtract)
                c.mark(("p6vF", i), "seqV")
            if 1 <= i <= NT:
                j = i - 1
                if j >= 2:
                    c.R["p6dS"].wait(v, j - 2)
                    if extra_out is not None:
                        c.R["dOx"].wait(v, j - 2)
                c.R["p6dX"].wait(v, j)
                for hf in range(2):
                    v.wait_ge(sems["p6tM"], j * NE + hf * 2 + 2)
                    for e2 in range(2):
                        e = hf * 2 + e2
                        sl = slice(e * 512, (e + 1) * 512)
                        v.tensor_tensor(x2ts[j % 2][:, sl], PSY[e][:, :],
                                        x1rs[j % 2][:, sl], ALU.add)
                    c.mark(("p6vOh", 2 * j + hf), "seqV")
                for jj in range(4):
                    v.bn_stats(bnb[:, 6 * jj:6 * jj + 6],
                               x2ts[j % 2][:, 512 * jj:512 * (jj + 1)])
                v.bn_aggr(mv3[:, 2 * j:2 * j + 2], bnb[:, :])
                v.tensor_scalar(nm3[:, j:j + 1], mv3[:, 2 * j:2 * j + 1], -1.0,
                                None, ALU.mult)
                v.tensor_scalar(x1rs[j % 2][:, :], x2ts[j % 2][:, :],
                                nm3[:, j:j + 1], None, ALU.add)
                v.tensor_reduce(am3[:, j:j + 1], x1rs[j % 2][:, :], AX.X, ALU.max,
                                apply_absolute_value=True)
                c.mark(("p6vO", j), "seqV")
                c.mark(("p6vS", j), "seqV")



def _host_prep(inputs, TL):
    x = np.ascontiguousarray(np.asarray(inputs["L0_attn_x"], dtype=np.float32))
    Wk = np.asarray(inputs["L0_attn_Wk"], dtype=np.float32)
    Wv = np.asarray(inputs["L0_attn_Wv"], dtype=np.float32)
    Wr = np.asarray(inputs["L0_attn_Wr"], dtype=np.float32)
    Wo = np.asarray(inputs["L0_attn_Wo"], dtype=np.float32)
    dec = np.ascontiguousarray(np.asarray(inputs["L0_attn_decay"],
                                          dtype=np.float32).reshape(1, D))
    gnw = np.ascontiguousarray(np.asarray(inputs["L0_attn_gn_weight"],
                                          dtype=np.float32).reshape(16, 128))
    gnb = np.ascontiguousarray(np.asarray(inputs["L0_attn_gn_bias"],
                                          dtype=np.float32).reshape(16, 128))
    Wf = np.asarray(inputs["L0_ffn_weight"], dtype=np.float32)
    ws = np.ascontiguousarray(np.asarray(inputs["L0_ffn_weight_scale"],
                                         dtype=np.float32).reshape(1, 1))
    bia = np.ascontiguousarray(np.asarray(inputs["L0_ffn_bias"],
                                          dtype=np.float32).reshape(1, D))

    def blkT(W):
        WT = np.ascontiguousarray(W.T)
        return np.ascontiguousarray(
            WT.reshape(NCH, 128, NCH, 128).transpose(2, 1, 0, 3).reshape(
                NCH, 128, NCH * 128))

    def rowT(W):
        WT = np.ascontiguousarray(W.T)
        return np.ascontiguousarray(WT.reshape(NCH, 128, D).transpose(1, 0, 2))

    wkb, wvb, wrb = blkT(Wk), blkT(Wv), blkT(Wr)
    wob, wfb = rowT(Wo), rowT(Wf)
    eye32 = np.eye(128, dtype=np.float32)
    eye16 = np.eye(128, dtype=ml_dtypes.bfloat16)
    onerow = np.ones((1, 128), np.float32)
    onecol = np.ones((128, 1), np.float32)

    in_maps = []
    for core in range(8):
        b, h = core // 2, core % 2
        in_maps.append({
            "x": np.ascontiguousarray(x[b, h * TL:(h + 1) * TL, :]),
            "wkb": wkb, "wvb": wvb, "wrb": wrb, "wob": wob, "wfb": wfb,
            "dec": dec, "gnw": gnw, "gnb": gnb, "bia": bia, "ws": ws,
            "hbase": np.full((128, 1), float(h * TL), np.float32),
            "isodd": np.full((128, 1), float(h == 1), np.float32),
            "iseven": np.full((128, 1), float(h == 0), np.float32),
            "eye32": eye32, "eye16": eye16, "onerow": onerow, "onecol": onecol,
        })
    return in_maps


def kernel(**inputs):
    TL = 2048
    in_maps = _host_prep(inputs, TL)
    nc = build_nc(stage="full", TL=TL)
    res = run_bass_kernel_spmd(nc, in_maps, core_ids=list(range(8)))
    out = np.empty((4, 2 * TL, D), dtype=np.float32)
    for core in range(8):
        b, h = core // 2, core % 2
        out[b, h * TL:(h + 1) * TL, :] = res.results[core]["y"]
    return out
